# version 06
# baseline (speedup 1.0000x reference)
import numpy as np
import ml_dtypes
import concourse.bass as bass
import concourse.mybir as mybir
from concourse.bass_utils import run_bass_kernel_spmd

F32 = mybir.dt.float32
BF16 = mybir.dt.bfloat16
AF = mybir.ActivationFunctionType
ALU = mybir.AluOpType
AX = mybir.AxisListType

NT = 2048
NG = 16
D = 1024
H = 16
DH = 64
NE = 17
INNER_LR = 0.01
EPS = 1e-5

DEBUG = []
CHUNK_STAGE = 99
TRUNC = None

ENGINES = ("sync", "tensor", "vector", "scalar", "gpsimd")


class B:
    def __init__(self, nc):
        self.nc = nc
        self.q = {e: [] for e in ENGINES}
        self.sem = {}
        self.cnt = {}

    def new_sem(self, name):
        self.sem[name] = self.nc.alloc_semaphore(name)
        self.cnt[name] = 0

    def op(self, eng, fn=None, wait=(), inc=()):
        for s, n in inc:
            self.cnt[s] += n
        self.q[eng].append((fn, tuple(wait), tuple(inc)))

    def dma(self, eng, out, in_, sem, wait=()):
        self.op(eng, lambda e, o=out, i=in_: e.dma_start(out=o, in_=i),
                wait=wait, inc=((sem, 16),))
        return self.cnt[sem]

    def flush(self):
        with self.nc.Block() as blk:
            for eng_name in ENGINES:
                ops = self.q[eng_name]
                if not ops:
                    continue

                def body(e, ops=ops):
                    for fn, waits, incs in ops:
                        for s, v in waits:
                            e.wait_ge(self.sem[s], v)
                        inst = fn(e) if fn is not None else None
                        if incs:
                            assert inst is not None
                            for s, n in incs:
                                inst.then_inc(self.sem[s], n)
                getattr(blk, eng_name)(body)


def cdiv(a, b):
    return (a + b - 1) // b


def build_program():
    nc = bass.Bass("TRN2", target_bir_lowering=False, debug=False)
    b = B(nc)
    for s in ("sc", "sx0", "sx1", "w0", "w1", "m0", "m1", "n0", "n1",
              "so0", "so1", "sg", "pe", "ve", "se", "po",
              "ex0", "ex1", "ex2", "ex3", "eo0", "eo1"):
        b.new_sem(s)

    di = lambda n, s: nc.dram_tensor(n, s, F32, kind="ExternalInput").ap()
    x_s = di("x_s", [NT, D])
    xp_s = di("xp_s", [NT, D])
    wqt = di("wqt", [D, D])
    wkt = di("wkt", [D, D])
    wvt = di("wvt", [D, D])
    wot = di("wot", [D, D])
    w1t = di("w1t", [D, 4 * D])
    w2t = di("w2t", [4 * D, D])
    w13t = di("w13t", [NE, D, D])
    w2et = di("w2et", [NE, 512, D])
    gate_wt = di("gate_wt", [D, 16])
    ln1w_c = di("ln1w_c", [128, 8])
    ln1b_c = di("ln1b_c", [128, 8])
    ln1w_row = di("ln1w_row", [1, D])
    ln1b_row = di("ln1b_row", [1, D])
    ln2w_row = di("ln2w_row", [1, D])
    ln2b_row = di("ln2b_row", [1, D])
    b1_c = di("b1_c", [128, 32])
    b2_row = di("b2_row", [1, D])
    gateb_col = di("gateb_col", [16, 1])
    gateb_row = di("gateb_row", [1, 16])
    lr_row = di("lr_row", [1, D])
    stril_in = di("stril", [128, 128])
    bdl_in = di("bdl", [128, 128])
    bdu_in = di("bdu", [128, 128])
    offu_in = di("offu", [128, 128])
    offl_in = di("offl", [128, 128])
    striu_in = di("striu", [128, 128])
    wflag = di("wflag", [128, 1])
    ident_in = nc.dram_tensor("ident_bf", [128, 128], BF16,
                              kind="ExternalInput").ap()
    identf_in = di("ident_f", [128, 128])
    out_s = nc.dram_tensor("out_s", [NT, D], F32, kind="ExternalOutput").ap()

    dbg = {}
    for name in DEBUG:
        dbg[name] = nc.dram_tensor("dbg_" + name, [128, 16384], F32,
                                   kind="ExternalOutput").ap()

    sb = nc.alloc_sbuf_tensor
    masks = sb("masks", [128, 6 * 128], F32)
    M_STRIL, M_BDL, M_BDU, M_OFFU, M_OFFL, M_STRIU = (0, 128, 256, 384,
                                                      512, 640)
    identb = sb("identb", [128, 128], BF16)
    identf = sb("identf", [128, 128], F32)
    eta16 = sb("eta16", [128, 16], F32)
    ln1w_cs = sb("ln1w_cs", [128, 8], F32)
    ln1b_cs = sb("ln1b_cs", [128, 8], F32)
    b1_cs = sb("b1_cs", [128, 32], F32)
    gwt_bf = sb("gwt_bf", [128, 8 * 16], BF16)
    gateb_cs = sb("gateb_cs", [16, 1], F32)
    gateb_b = sb("gateb_b", [128, 16], F32)
    mu_cols = sb("mu_cols", [128, NG], F32)
    rstd_cols = sb("rstd_cols", [128, NG], F32)
    wst = sb("wst", [128, 8 * DH], F32)
    wst_bf = sb("wst_bf", [128, 8 * DH], BF16)
    eps_cell = sb("eps_cell", [128, 1], F32)
    ones_bf = sb("ones_bf", [1, 128], BF16)
    ones_f = sb("ones_f", [1, 128], F32)
    onesc_bf = sb("onesc_bf", [128, 1], BF16)
    flag_cell = sb("flag_cell", [128, 1], F32)
    ln_sum = sb("ln_sum", [128, 4], F32)
    ln_row = sb("ln_row", [1, 8 * 4], F32)
    wstage = sb("wstage", [128, 2 * 1024], F32)

    AR = 186 * 1024
    arena = sb("arena", [128, AR], mybir.dt.uint8)

    def ar(off_kb, nbytes, dt):
        off = int(off_kb * 1024)
        assert off % 4 == 0 and off + nbytes <= AR, (off_kb, nbytes)
        return arena[:, off:off + nbytes].bitcast(dt)

    pA = nc.alloc_psum_tensor("pA", [128, 2048], F32)
    pB = nc.alloc_psum_tensor("pB", [128, 2048], F32)
    bank_evac = {}
    K8 = ["A0", "A1", "A2", "A3", "B0", "B1", "B2", "B3"]

    def q8(i):
        t = pA if i < 4 else pB
        return t[:, (i % 4) * 512:(i % 4) * 512 + 512]

    def bq(i):
        return q8(i)

    def pstr(t, w):
        return t[:].rearrange("p (j q) -> p j q", q=512)[:, :, 0:w]

    def bwait(*keys):
        return tuple(bank_evac[k] for k in keys if k in bank_evac)

    def bdone(k, sem):
        bank_evac[k] = (sem, b.cnt[sem])

    b.dma("sync", masks[:, 0:128], stril_in, "sc")
    b.dma("sync", masks[:, 128:256], bdl_in, "sc")
    b.dma("sync", masks[:, 256:384], bdu_in, "sc")
    b.dma("sync", masks[:, 384:512], offu_in, "sc")
    b.dma("sync", masks[:, 512:640], offl_in, "sc")
    b.dma("sync", masks[:, 640:768], striu_in, "sc")
    b.dma("sync", identb[:], ident_in, "sc")
    b.dma("sync", identf[:], identf_in, "sc")
    b.dma("sync", ln1w_cs[:], ln1w_c, "sc")
    b.dma("sync", ln1b_cs[:], ln1b_c, "sc")
    b.dma("sync", b1_cs[:], b1_c, "sc")
    b.dma("sync", gateb_cs[:], gateb_col, "sc")
    b.dma("sync", flag_cell[:], wflag, "sc")
    b.op("vector", lambda e: e.memset(eps_cell[:], EPS))
    b.op("vector", lambda e: e.memset(ones_bf[:], 1.0))
    b.op("vector", lambda e: e.memset(ones_f[:], 1.0))
    b.op("vector", lambda e: e.memset(onesc_bf[:], 1.0))
    b.op("vector", lambda e: e.memset(wst[:], 0.0))
    b.op("vector", lambda e: e.memset(wst_bf[:], 0.0), inc=(("ve", 1),))

    def bc_mm(src_ap, dst_tile, w, wait_first, dst_off=0):
        for hf in range(cdiv(w, 512)):
            ww = min(512, w - hf * 512)
            k = "A" + str(hf % 2)
            bk = bq(hf % 2)
            b.op("tensor",
                 lambda e, s=src_ap, hw=hf * 512, ww=ww, bk=bk:
                 e.matmul(bk[:, 0:ww], ones_f[:], s[0:1, hw:hw + ww],
                          start=True, stop=True),
                 wait=(wait_first if hf == 0 else ()) + bwait(k),
                 inc=(("pe", 1),))
            pv = b.cnt["pe"]
            b.op("scalar",
                 lambda e, d=dst_tile, hw=hf * 512, ww=ww, bk=bk, o=dst_off:
                 e.copy(out=d[:, o + hw:o + hw + ww], in_=bk[:, 0:ww]),
                 wait=(("pe", pv),), inc=(("se", 1),))
            bdone(k, "se")
        return b.cnt["se"]

    rowbuf = ar(0, 8192, F32)
    b.dma("sync", rowbuf[0:1, 0:D], lr_row, "sc")
    gwr = gate_wt.rearrange("(c p) n -> p c n", p=128)
    b.dma("sync", wstage[:, 0:128].rearrange("p (c n) -> p c n", n=16),
          gwr, "sc")
    b.dma("sync", rowbuf[0:1, D + 16:D + 32], gateb_row, "sc")
    sc_all = b.cnt["sc"]
    b.op("vector", lambda e: e.tensor_scalar_mul(out=rowbuf[0:1, 0:D],
                                                 in0=rowbuf[0:1, 0:D],
                                                 scalar1=INNER_LR),
         wait=(("sc", sc_all),), inc=(("ve", 1),))
    b.op("vector",
         lambda e: e.tensor_copy(
             out=rowbuf[0:1, D:D + 16].rearrange("p (h r) -> p h r", r=1),
             in_=rowbuf[0:1, 0:D].rearrange("p (h r) -> p h r", r=64)[
                 :, :, 0:1]),
         wait=(("ve", b.cnt["ve"]),), inc=(("ve", 1),))
    bc_mm(rowbuf[0:1, D:D + 16], eta16, 16, (("ve", b.cnt["ve"]),))
    bc_mm(rowbuf[0:1, D + 16:D + 32], gateb_b, 16,
          (("sc", sc_all),))
    b.op("scalar", lambda e: e.copy(out=gwt_bf[:], in_=wstage[:, 0:128]),
         wait=(("sc", sc_all),), inc=(("se", 1),))
    se_p0a = b.cnt["se"]

    def eta_rep(w):
        return eta16[:, 0:16].unsqueeze(2).broadcast_to((128, 16, w))

    def mask_rep(off, n_h=16):
        return masks[:, off:off + 128].unsqueeze(1).broadcast_to(
            (128, n_h, 128))

    xnf = ar(0, 32768, BF16)
    xnf3 = xnf.rearrange("p (c t) -> p c t", t=NT)

    def p0_stage(i):
        return ar(32 + 4 * i, 4096, F32)

    def p0_tok(i):
        return ar(40 + 2 * i, 2048, BF16)

    xr = x_s.rearrange("(g p) d -> g p d", p=128)
    orr = out_s.rearrange("(g p) d -> g p d", p=128)

    def emit_ln_group(xg, mu, rstd, wait_in, xn_out_bf, extra=()):
        b.op("vector", lambda e: e.reduce_sum(out=ln_sum[:, 0:1], in_=xg,
                                              axis=AX.X),
             wait=wait_in, inc=(("ve", 1),))
        b.op("scalar", lambda e: e.activation(out=xn_out_bf, in_=xg,
                                              func=AF.Square,
                                              accum_out=ln_sum[:, 1:2]),
             wait=wait_in + tuple(extra), inc=(("se", 1),))
        vs, ss = b.cnt["ve"], b.cnt["se"]
        w = xg.shape[-1]
        b.op("vector", lambda e: e.tensor_scalar_mul(out=mu,
                                                     in0=ln_sum[:, 0:1],
                                                     scalar1=1.0 / w),
             wait=(("ve", vs),), inc=(("ve", 1),))
        b.op("vector", lambda e: e.tensor_scalar_mul(out=ln_sum[:, 2:3],
                                                     in0=ln_sum[:, 1:2],
                                                     scalar1=1.0 / w),
             wait=(("se", ss), ("ve", b.cnt["ve"])), inc=(("ve", 1),))
        b.op("vector", lambda e: e.tensor_mul(out=ln_sum[:, 3:4], in0=mu,
                                              in1=mu),
             wait=(("ve", b.cnt["ve"]),), inc=(("ve", 1),))
        b.op("vector", lambda e: e.tensor_sub(out=ln_sum[:, 2:3],
                                              in0=ln_sum[:, 2:3],
                                              in1=ln_sum[:, 3:4]),
             wait=(("ve", b.cnt["ve"]),), inc=(("ve", 1),))
        b.op("scalar", lambda e: e.activation(out=ln_sum[:, 3:4],
                                              in_=ln_sum[:, 2:3],
                                              func=AF.Sqrt,
                                              bias=eps_cell[:]),
             wait=(("ve", b.cnt["ve"]),), inc=(("se", 1),))
        b.op("vector", lambda e: e.reciprocal(out=rstd, in_=ln_sum[:, 3:4]),
             wait=(("se", b.cnt["se"]),), inc=(("ve", 1),))
        b.op("vector", lambda e: e.tensor_scalar(out=xn_out_bf, in0=xg,
                                                 scalar1=mu, scalar2=rstd,
                                                 op0=ALU.subtract,
                                                 op1=ALU.mult),
             wait=(("ve", b.cnt["ve"]),) + tuple(extra), inc=(("ve", 1),))
        return b.cnt["ve"]

    def emit_tr_group(g, src_bf, dst3, ve_ready, key, bank_bf, extra=()):
        for c in range(8):
            b.op("tensor",
                 lambda e, c=c: e.matmul(
                     bank_bf[:, c * 128:(c + 1) * 128],
                     src_bf[:, c * 128:(c + 1) * 128], identb[:],
                     is_transpose=True, start=True, stop=True),
                 wait=(((("ve", ve_ready),) + tuple(extra) + bwait(key))
                       if c == 0 else ()),
                 inc=(("pe", 1),) if c == 7 else ())
        pev = b.cnt["pe"]
        b.op("vector",
             lambda e: e.tensor_copy(
                 out=dst3[:, :, g * 128:(g + 1) * 128],
                 in_=bank_bf[:].rearrange("p (c t) -> p c t", t=128)),
             wait=(("pe", pev),), inc=(("ve", 1),))
        bdone(key, "ve")
        return b.cnt["ve"]

    ve_ln_done, pe_tr = {}, {}
    for g in range(NG):
        sl_ = g % 2
        sxs = "sx" + str(sl_)
        wex = ((("ve", ve_ln_done[g - 2]), (sxs, b.cnt[sxs]))
               if g >= 2 else ())
        b.dma("sync", p0_stage(sl_), xr[g], sxs, wait=wex)
        ldg = b.cnt[sxs]
        xw = (("pe", pe_tr[g - 2]),) if g >= 2 else ()
        vd = emit_ln_group(p0_stage(sl_), mu_cols[:, g:g + 1],
                           rstd_cols[:, g:g + 1], ((sxs, ldg),),
                           p0_tok(sl_), extra=xw)
        ve_ln_done[g] = vd
        exw = (("se", se_p0a),) if g < 2 else ()
        emit_tr_group(g, p0_tok(sl_), xnf3, vd, "B" + str(sl_),
                      q8(4 + sl_).bitcast(BF16), extra=exw)
        pe_tr[g] = b.cnt["pe"]
    ve_p0 = b.cnt["ve"]

    dbg_stage = sb("dbg_stage", [128, 1024], F32) if DEBUG else None

    def dump(name, src_fn, n_cols, wait):
        for i in range(cdiv(n_cols, 1024)):
            b.op("scalar",
                 lambda e, i=i: e.copy(out=dbg_stage[:, 0:1024],
                                       in_=src_fn(i)),
                 wait=tuple(wait) + (("sg", b.cnt["sg"]),), inc=(("se", 1),))
            b.dma("gpsimd", dbg[name][:, i * 1024:(i + 1) * 1024],
                  dbg_stage[:, 0:1024], "sg", wait=(("se", b.cnt["se"]),))
        b.op("gpsimd", None, wait=(("sg", b.cnt["sg"]),))

    if "xnf" in DEBUG:
        dump("xnf", lambda i: xnf[:, i * 1024:(i + 1) * 1024], 16384,
             [("ve", ve_p0)])
    if TRUNC == "p0":
        b.flush()
        return nc

    def wslice(t, h):
        return t[(h % 2) * 64:(h % 2) * 64 + 64,
                 (h // 2) * 64:(h // 2) * 64 + 64]

    def mk_ttt(ws):
        a_, bb_, cc_, dd_ = ws["sa"], ws["sb"], ws["sc"], ws["sd"]
        ee_, ff_, gg_ = ws["se_"], ws["sf"], ws["sg_"]
        P = ws["ttblk"]
        evq = ws["evq"]
        rr = ws["rr"]

        def grp_banks(g):
            return ("A0", "A1", "A2", "A3") if g == 0 else (
                "B0", "B1", "B2", "B3")

        def grp_t(g):
            return pA if g == 0 else pB

        def qview(t, q, w, n=4):
            return t[:, q * 4 * w:(q * 4 + n) * w].rearrange(
                "p (j x) -> p j x", x=w)

        def dep(key, q):
            return (evq[(key, q)],) if (key, q) in evq else ()

        def depall(key):
            best = {}
            for q in range(4):
                if (key, q) in evq:
                    s, v = evq[(key, q)]
                    best[s] = max(best.get(s, 0), v)
            return tuple(best.items())

        def round4(dst_key, q, mm_fn, evac_fn, deps, w):
            g = rr[0] % 2
            rr[0] += 1
            for j in range(4):
                wt = ()
                if j == 0:
                    wt = bwait(*grp_banks(g)) + tuple(deps)
                mm_fn(j, q8(4 * g + j)[:, 0:w], wt, j == 3)
            evac_fn(g, q, w)

        EV_CYCLE = ("scalar", "vector", "scalar")
        EV_SEM = {"vector": "ve", "scalar": "se", "gpsimd": "po"}

        def ev_to(dst, dst_key, eng_alt=True):
            def f(g, q, w):
                eng = EV_CYCLE[ws["er"][0] % 3] if eng_alt else "vector"
                ws["er"][0] += 1
                sem = EV_SEM[eng]
                if eng == "scalar":
                    b.op("scalar",
                         lambda e, g=g, q=q, w=w: e.copy(
                             out=qview(dst, q, w), in_=pstr(grp_t(g), w)),
                         wait=(("pe", b.cnt["pe"]),),
                         inc=((sem, 1),))
                else:
                    b.op(eng,
                         lambda e, g=g, q=q, w=w: e.tensor_copy(
                             out=qview(dst, q, w), in_=pstr(grp_t(g), w)),
                         wait=(("pe", b.cnt["pe"]),), inc=((sem, 1),))
                evq[(dst_key, q)] = (sem, b.cnt[sem])
                for k in grp_banks(g):
                    bdone(k, sem)
            return f

        def hs4(t, q, j):
            return t[:, (4 * q + j) * 128:(4 * q + j) * 128 + 128]

        def hs(t, h):
            return t[:, h * 128:(h + 1) * 128]

        POST_R1 = 8
        POST_U = 28

        def plan_A(c, Kf, Qf, state_only):
            th = []
            post_o = POST_U if state_only else POST_U + 2
            kn_t = ws["kn"][c % 2]

            def kn_round(c=c, Kf=Kf, kn_t=kn_t):
                knb = ws["kn_psum"]
                for p in range(8):
                    b.op("tensor",
                         lambda e, p=p: e.matmul(
                             knb[:, p * 128:(p + 1) * 128], Kf(p),
                             identb[:], is_transpose=True, start=True,
                             stop=True),
                         wait=(bwait(ws["kn_key"]) + tuple(ws["kf_ready"])
                               ) if p == 0 else (),
                         inc=(("pe", 1),) if p == 7 else ())
                b.op("scalar",
                     lambda e: e.copy(out=kn_t[:], in_=knb[:]),
                     wait=(("pe", b.cnt["pe"]),), inc=(("se", 1),))
                bdone(ws["kn_key"], "se")
                ws["ve_kn"][c % 2] = ("se", b.cnt["se"])
            th.append((0, kn_round))

            for q in range(4):
                def gq(q=q, Kf=Kf):
                    def gmm(j, out, wt, last, q=q):
                        h = 4 * q + j
                        b.op("tensor",
                             lambda e, h=h, out=out: e.matmul(
                                 out, Kf2(Kf, h), Kf2(Kf, h), start=True,
                                 stop=True),
                             wait=wt, inc=(("pe", 1),) if last else ())

                    def gev(g, q, w):
                        b.op("vector",
                             lambda e, g=g, q=q, w=w: e.tensor_tensor(
                                 out=qview(a_, q, w), in0=pstr(grp_t(g), w),
                                 in1=eta16[:, 4 * q:4 * q + 4].unsqueeze(
                                     2).broadcast_to((128, 4, w)),
                                 op=ALU.mult),
                             wait=(("pe", b.cnt["pe"]),), inc=(("ve", 1),))
                        evq[("ng", q)] = ("ve", b.cnt["ve"])
                        for k in grp_banks(g):
                            bdone(k, "ve")
                        b.op("gpsimd",
                             lambda e, q=q: e.tensor_tensor(
                                 out=qview(bb_, q, 128),
                                 in0=qview(a_, q, 128),
                                 in1=mask_rep(M_BDL, 4), op=ALU.mult),
                             wait=(("ve", b.cnt["ve"]),
                                   ("po", b.cnt["po"])),
                             inc=(("po", 1),))
                        evq[("D", q)] = ("po", b.cnt["po"])
                        b.op("gpsimd",
                             lambda e, q=q: e.tensor_tensor(
                                 out=qview(cc_, q, 128),
                                 in0=qview(a_, q, 128),
                                 in1=mask_rep(M_BDU, 4), op=ALU.mult),
                             wait=(("po", b.cnt["po"]),), inc=(("po", 1),))
                        evq[("Dt", q)] = ("po", b.cnt["po"])
                        b.op("gpsimd",
                             lambda e, q=q: e.tensor_tensor(
                                 out=qview(dd_, q, 128),
                                 in0=qview(a_, q, 128),
                                 in1=mask_rep(M_OFFL, 4), op=ALU.mult),
                             wait=(("po", b.cnt["po"]),), inc=(("po", 1),))
                        evq[("Loff", q)] = ("po", b.cnt["po"])

                    round4("ng", q, gmm, gev, ws["kf_ready"], 128)
                th.append((0, gq))

            chain = [
                ("D2", ee_, "Dt", cc_, "D", bb_),
                ("D2t", ff_, "D", bb_, "Dt", cc_),
                ("D4", gg_, "D2t", ff_, "D2", ee_),
                ("D4t", cc_, "D2", ee_, "D2t", ff_),
                ("D8", ff_, "D4t", cc_, "D4", gg_),
                ("D8t", a_, "D4", gg_, "D4t", cc_),
                ("Dt16", cc_, "D8", ff_, "D8t", a_),
            ]
            for dk, dt_, lk, lt, rk, rt_ in chain:
                for q in range(4):
                    def cq(dk=dk, dt_=dt_, lk=lk, lt=lt, rk=rk, rt_=rt_,
                           q=q):
                        def smm(j, out, wt, last, lt=lt, rt_=rt_, q=q):
                            b.op("tensor",
                                 lambda e, j=j, out=out, lt=lt, rt_=rt_,
                                 q=q: e.matmul(
                                     out, hs4(lt, q, j), hs4(rt_, q, j),
                                     start=True, stop=True),
                                 wait=wt,
                                 inc=(("pe", 1),) if last else ())
                        xtra = dep("Loff", q) if dk == "D8t" else ()
                        round4(dk, q, smm, ev_to(dt_, dk),
                               dep(lk, q) + dep(rk, q) + xtra, 128)
                    th.append((0, cq))

            for q in range(4):
                def piq(q=q):
                    b.op("vector",
                         lambda e, q=q: e.tensor_tensor(
                             out=qview(P, q, 128), in0=qview(cc_, q, 128),
                             in1=identb[:].unsqueeze(1).broadcast_to(
                                 (128, 4, 128)), op=ALU.add),
                         wait=dep("Dt16", q) + (("ve", b.cnt["ve"]),
                                                ("pe", b.cnt["pe"])),
                         inc=(("ve", 1),))
                    evq[("P", q)] = ("ve", b.cnt["ve"])
                th.append((POST_R1, piq))
            for lk, lt in (("D8", ff_), ("D4", gg_), ("D2", ee_)):
                for q in range(4):
                    def ppq(lk=lk, lt=lt, q=q):
                        def pmm(j, out, wt, last, lt=lt, q=q):
                            b.op("tensor",
                                 lambda e, j=j, out=out, lt=lt, q=q:
                                 e.matmul(out, hs4(lt, q, j), hs4(P, q, j),
                                          start=True, stop=True),
                                 wait=wt,
                                 inc=(("pe", 1),) if last else ())

                        def pev(g, q, w):
                            pe_ = "vector"
                            ps_ = EV_SEM[pe_]
                            b.op(pe_,
                                 lambda e, g=g, q=q, w=w: e.tensor_tensor(
                                     out=qview(P, q, w), in0=qview(P, q, w),
                                     in1=pstr(grp_t(g), w), op=ALU.add),
                                 wait=(("pe", b.cnt["pe"]),) + dep("P", q),
                                 inc=((ps_, 1),))
                            evq[("P", q)] = (ps_, b.cnt[ps_])
                            for k in grp_banks(g):
                                bdone(k, ps_)
                        round4("P", q, pmm, pev, dep(lk, q) + dep("P", q),
                               128)
                    th.append((POST_R1, ppq))
            for q in range(4):
                def pfq(q=q):
                    def fmm(j, out, wt, last, q=q):
                        b.op("tensor",
                             lambda e, j=j, out=out, q=q: e.matmul(
                                 out, hs4(bb_, q, j), hs4(P, q, j),
                                 start=True, stop=True),
                             wait=wt, inc=(("pe", 1),) if last else ())

                    def fev(g, q, w):
                        pe_ = "vector"
                        ps_ = EV_SEM[pe_]
                        b.op(pe_,
                             lambda e, g=g, q=q, w=w: e.tensor_tensor(
                                 out=qview(P, q, w), in0=qview(P, q, w),
                                 in1=pstr(grp_t(g), w), op=ALU.subtract),
                             wait=(("pe", b.cnt["pe"]),) + dep("P", q),
                             inc=((ps_, 1),))
                        evq[("P", q)] = (ps_, b.cnt[ps_])
                        for k in grp_banks(g):
                            bdone(k, ps_)
                    round4("P", q, fmm, fev, dep("D", q) + dep("P", q), 128)
                th.append((POST_R1, pfq))
            for q in range(4):
                def mhq(q=q):
                    def mmm(j, out, wt, last, q=q):
                        b.op("tensor",
                             lambda e, j=j, out=out, q=q: e.matmul(
                                 out, hs4(dd_, q, j), hs4(P, q, j),
                                 start=True, stop=True),
                             wait=wt, inc=(("pe", 1),) if last else ())
                    round4("Mh", q, mmm, ev_to(ws["mhat"], "Mh"),
                           dep("Loff", q) + dep("P", q), 128)
                th.append((POST_U, mhq))
            if not state_only:
                for q in range(4):
                    def stq(q=q, Kf=Kf, Qf=Qf):
                        def stmm(j, out, wt, last, q=q):
                            h = 4 * q + j
                            b.op("tensor",
                                 lambda e, h=h, out=out: e.matmul(
                                     out, Kf2(Kf, h), Kf2(Qf, h),
                                     start=True, stop=True),
                                 wait=wt, inc=(("pe", 1),) if last else ())

                        def stev(g, q, w):
                            b.op("vector",
                                 lambda e, g=g, q=q, w=w: e.tensor_tensor(
                                     out=qview(ws["st"], q, w),
                                     in0=pstr(grp_t(g), w),
                                     in1=masks[:, M_STRIU:M_STRIU + 128
                                               ].unsqueeze(1).broadcast_to(
                                         (128, 4, w)), op=ALU.mult),
                                 wait=(("pe", b.cnt["pe"]),),
                                 inc=(("ve", 1),))
                            evq[("St", q)] = ("ve", b.cnt["ve"])
                            for k in grp_banks(g):
                                bdone(k, "ve")
                        round4("St", q, stmm, stev, ws["kf_ready"], 128)
                    th.append((post_o, stq))
            return th

        def plan_B(c, Kf, Vt, Qf, Ofun, state_only):
            th = []
            x0, x1, x2, x3 = ws["x0"], ws["x1"], ws["x2"], ws["x3"]
            kn_t = ws["kn"][c % 2]

            for q in range(4):
                def rq(q=q, Kf=Kf, Vt=Vt):
                    def rmm(j, out, wt, last, q=q):
                        h = 4 * q + j
                        b.op("tensor",
                             lambda e, h=h, out=out: e.matmul(
                                 out, Kf2(Kf, h), wslice(wst_bf, h),
                                 start=True, stop=True),
                             wait=wt, inc=(("pe", 1),) if last else ())

                    def rev(g, q, w):
                        b.op("vector",
                             lambda e, g=g, q=q, w=w: e.tensor_tensor(
                                 out=qview(x0, q, w), in0=pstr(grp_t(g), w),
                                 in1=eta16[:, 4 * q:4 * q + 4].unsqueeze(
                                     2).broadcast_to((128, 4, w)),
                                 op=ALU.mult),
                             wait=(("pe", b.cnt["pe"]),), inc=(("ve", 1),))
                        for k in grp_banks(g):
                            bdone(k, "ve")
                        b.op("vector",
                             lambda e, q=q, w=w: e.tensor_sub(
                                 out=x0[:, q * 256:(q + 1) * 256],
                                 in0=Vt[:, q * 256:(q + 1) * 256],
                                 in1=x0[:, q * 256:(q + 1) * 256]),
                             wait=(("ve", b.cnt["ve"]),) +
                             tuple(ws["vt_ready"]),
                             inc=(("ve", 1),))
                        evq[("R", q)] = ("ve", b.cnt["ve"])
                    round4("R", q, rmm, rev,
                           tuple(ws["w_ready"]) +
                           (("ve", ws["w_last_ve"][0]),), 64)
                th.append(rq)

            def apply_q(dst_key, dst, lam_t, lam_key, rho, rho_key, q,
                        extra_key=None):
                def amm(j, out, wt, last, q=q):
                    h = 4 * q + j
                    b.op("tensor",
                         lambda e, h=h, out=out: e.matmul(
                             out, hs(lam_t, h),
                             rho[:, h * 64:(h + 1) * 64], start=True,
                             stop=True),
                         wait=wt, inc=(("pe", 1),) if last else ())
                xtra = dep(extra_key, q) if extra_key else ()
                round4(dst_key, q, amm, ev_to(dst, dst_key),
                       dep(lam_key, q) + dep(rho_key, q) + xtra, 64)

            for q in range(4):
                th.append(lambda q=q: apply_q("R1", x1, P, "P", x0, "R", q))
            for q in range(4):
                th.append(lambda q=q: apply_q("b1", x2, ws["mhat"], "Mh",
                                              x1, "R1", q))
            for q in range(4):
                def zq(q=q):
                    b.op("gpsimd",
                         lambda e, q=q: e.tensor_sub(
                             out=x0[:, q * 256:(q + 1) * 256],
                             in0=x1[:, q * 256:(q + 1) * 256],
                             in1=x2[:, q * 256:(q + 1) * 256]),
                         wait=dep("b1", q) + dep("R1", q) +
                         (("po", b.cnt["po"]),),
                         inc=(("po", 1),))
                    evq[("z", q)] = ("po", b.cnt["po"])
                th.append(zq)
            for q in range(4):
                th.append(lambda q=q: apply_q("b2", x2, ws["mhat"], "Mh",
                                              x2, "b1", q, extra_key="z"))
            for q in range(4):
                def z2q(q=q):
                    b.op("gpsimd",
                         lambda e, q=q: e.tensor_add(
                             out=x0[:, q * 256:(q + 1) * 256],
                             in0=x0[:, q * 256:(q + 1) * 256],
                             in1=x2[:, q * 256:(q + 1) * 256]),
                         wait=dep("b2", q) + dep("z", q) +
                         (("po", b.cnt["po"]),),
                         inc=(("po", 1),))
                    evq[("z2", q)] = ("po", b.cnt["po"])
                th.append(z2q)
            for q in range(4):
                def uq(q=q):
                    def umm(j, out, wt, last, q=q):
                        h = 4 * q + j
                        b.op("tensor",
                             lambda e, h=h, out=out: e.matmul(
                                 out, hs(ws["mhat"], h),
                                 x2[:, h * 64:(h + 1) * 64], start=True,
                                 stop=True),
                             wait=wt, inc=(("pe", 1),) if last else ())

                    def uev(g, q, w):
                        b.op("vector",
                             lambda e, g=g, q=q, w=w: e.tensor_tensor(
                                 out=qview(x3, q, w), in0=qview(x0, q, w),
                                 in1=pstr(grp_t(g), w), op=ALU.subtract),
                             wait=(("pe", b.cnt["pe"]),) + dep("z2", q),
                             inc=(("ve", 1),))
                        evq[("U", q)] = ("ve", b.cnt["ve"])
                        for k in grp_banks(g):
                            bdone(k, "ve")
                    round4("U", q, umm, uev, dep("Mh", q) + dep("b2", q),
                           64)
                th.append(uq)

            if not state_only:
                for half in range(2):
                    def oh(half=half, Qf=Qf, Ofun=Ofun):
                        g = rr[0] % 2
                        rr[0] += 1
                        for pp in range(4):
                            p = half * 4 + pp
                            he, ho = 2 * p, 2 * p + 1
                            reg = q8(4 * g + pp)[:, 0:128]
                            wt = ()
                            if pp == 0:
                                wt = bwait(*grp_banks(g)) + depall("U") + \
                                    depall("St")
                            b.op("tensor",
                                 lambda e, he=he, reg=reg: e.matmul(
                                     reg[0:64, :], wslice(wst_bf, he),
                                     Kf2(Qf, he), start=True, stop=False,
                                     tile_position=(0, 0)),
                                 wait=wt)
                            b.op("tensor",
                                 lambda e, ho=ho, reg=reg: e.matmul(
                                     reg[64:128, :], wslice(wst_bf, ho),
                                     Kf2(Qf, ho), start=True, stop=False,
                                     tile_position=(64, 64),
                                     skip_group_check=True))
                            b.op("tensor",
                                 lambda e, he=he, reg=reg: e.matmul(
                                     reg[0:64, :],
                                     x3[:, he * 64:(he + 1) * 64],
                                     hs(ws["st"], he), start=False,
                                     stop=True, tile_position=(0, 0)))
                            b.op("tensor",
                                 lambda e, ho=ho, reg=reg: e.matmul(
                                     reg[64:128, :],
                                     x3[:, ho * 64:(ho + 1) * 64],
                                     hs(ws["st"], ho), start=False,
                                     stop=True, tile_position=(0, 64),
                                     skip_group_check=True),
                                 inc=(("pe", 1),) if pp == 3 else ())
                        Ofun(b.cnt["pe"], half, grp_t(g))
                        for k in grp_banks(g):
                            bdone(k, "ve")
                    th.append(oh)

            for half in range(2):
                def wh(half=half, c=c, kn_t=kn_t):
                    g = rr[0] % 2
                    rr[0] += 1
                    for pp in range(4):
                        p = half * 4 + pp
                        he, ho = 2 * p, 2 * p + 1
                        reg = q8(4 * g + pp)[:, 0:64]
                        wt = ()
                        if pp == 0:
                            wt = bwait(*grp_banks(g)) + depall("U") + (
                                ws["ve_kn"][c % 2],)
                        b.op("tensor",
                             lambda e, he=he, reg=reg: e.matmul(
                                 reg[0:64, :],
                                 kn_t[:, (he // 2) * 128:
                                      (he // 2) * 128 + 64],
                                 x3[:, he * 64:(he + 1) * 64],
                                 start=True, stop=True,
                                 tile_position=(0, 0)),
                             wait=wt)
                        b.op("tensor",
                             lambda e, ho=ho, reg=reg: e.matmul(
                                 reg[64:128, :],
                                 kn_t[:, (ho // 2) * 128 + 64:
                                      (ho // 2) * 128 + 128],
                                 x3[:, ho * 64:(ho + 1) * 64],
                                 start=True, stop=True,
                                 tile_position=(0, 64)),
                             inc=(("pe", 1),) if pp == 3 else ())
                    pv = b.cnt["pe"]
                    b.op("vector",
                         lambda e, g=g, half=half: e.tensor_tensor(
                             out=wst[:, half * 256:(half + 1) * 256
                                     ].rearrange("p (j x) -> p j x", x=64),
                             in0=wst[:, half * 256:(half + 1) * 256
                                     ].rearrange("p (j x) -> p j x", x=64),
                             in1=pstr(grp_t(g), 64), op=ALU.add),
                         wait=(("pe", pv), ("ve", ws["w_last_ve"][0]))
                         if half == 0 else (("pe", pv),
                                            ("ve", b.cnt["ve"])),
                         inc=(("ve", 1),))
                    for k in grp_banks(g):
                        bdone(k, "ve")
                    if half == 1:
                        b.op("vector",
                             lambda e: e.tensor_copy(out=wst_bf[:],
                                                     in_=wst[:]),
                             wait=(("ve", b.cnt["ve"]),), inc=(("ve", 1),))
                        ws["w_last_ve"][0] = b.cnt["ve"]
                th.append(wh)
            return th

        return plan_A, plan_B

    def run_pair(Bth, Ath):
        ai = 0
        nB = len(Bth)
        for bi in range(nB):
            Bth[bi]()
            done = bi + 1
            rem_b = nB - done
            if rem_b == 0:
                break
            quota = cdiv(len(Ath) - ai, rem_b + 1)
            while quota > 0 and ai < len(Ath) and Ath[ai][0] <= done:
                Ath[ai][1]()
                ai += 1
                quota -= 1
        while ai < len(Ath):
            Ath[ai][1]()
            ai += 1

    def Kf2(Kf, h):
        t = Kf(h // 2)
        return t[(h % 2) * 64:(h % 2) * 64 + 64, :]

    def mk_ws(base_kb, kn_key):
        o = base_kb
        ws = {}
        for nm in ("sa", "sb", "sc", "sd", "se_", "sf", "sg_"):
            ws[nm] = ar(o, 4096, BF16)
            o += 4
        ws["ttblk"] = ar(o, 4096, BF16); o += 4
        ws["mhat"] = ar(o, 4096, BF16); o += 4
        ws["st"] = ar(o, 4096, BF16); o += 4
        for nm in ("x0", "x1", "x2", "x3"):
            ws[nm] = ar(o, 2048, BF16)
            o += 2
        ws["kn"] = [ar(o, 2048, BF16), ar(o + 2, 2048, BF16)]
        o += 4
        ws["ve_kn"] = [("se", 0), ("se", 0)]
        ws["evq"] = {}
        ws["rr"] = [0]
        ws["er"] = [0]
        ws["kn_key"] = kn_key
        ws["kn_psum"] = q8(K8.index(kn_key)).bitcast(BF16)
        ws["w_last_ve"] = [0]
        ws["end_kb"] = o
        return ws

    ln1wb = ar(32, 4096, F32)
    ln1bb = ar(36, 4096, F32)
    rowbuf_b = ar(40, 8192, F32)
    b.dma("sync", rowbuf_b[0:1, 0:D], ln1w_row, "sc",
          wait=(("se", se_p0a), ("ve", ve_p0)))
    b.dma("sync", rowbuf_b[0:1, D:2 * D], ln1b_row, "sc")
    ldw = b.cnt["sc"]
    bc_mm(rowbuf_b[0:1, 0:D], ln1wb, D, (("sc", ldw), ("ve", ve_p0)))
    bc_mm(rowbuf_b[0:1, D:2 * D], ln1bb, D, (("sc", ldw),))
    se_ln1b = b.cnt["se"]

    wkt_p0b = ar(80, 16384, BF16)
    wvt_p0b = ar(96, 16384, BF16)

    def stream_weight_1024(dram_t, dst_bf, wait0, eta_scale=False):
        wr = dram_t.rearrange("(c p) o -> c p o", p=128)
        ses = []
        for c in range(8):
            slot = c % 2
            ws_ = "w" + str(slot)
            wv = [("se", ses[c - 2])] if c >= 2 else []
            if c >= 2:
                wv.append((ws_, b.cnt[ws_]))
            b.dma("sync", wstage[:, slot * 1024:(slot + 1) * 1024], wr[c],
                  ws_, wait=tuple(wv) + (tuple(wait0) if c <= 1 else ()))
            ld = b.cnt[ws_]
            if eta_scale:
                b.op("vector",
                     lambda e, c=c, slot=slot: e.tensor_tensor(
                         out=wstage[:, slot * 1024:(slot + 1) * 1024
                                    ].rearrange("p (h t) -> p h t", t=64),
                         in0=wstage[:, slot * 1024:(slot + 1) * 1024
                                    ].rearrange("p (h t) -> p h t", t=64),
                         in1=eta_rep(64), op=ALU.mult),
                     wait=((ws_, ld),), inc=(("ve", 1),))
                wv2 = (("ve", b.cnt["ve"]),)
            else:
                wv2 = ((ws_, ld),)
            b.op("scalar",
                 lambda e, c=c, slot=slot: e.copy(
                     out=dst_bf[:, c * 1024:(c + 1) * 1024],
                     in_=wstage[:, slot * 1024:(slot + 1) * 1024]),
                 wait=wv2, inc=(("se", 1),))
            ses.append(b.cnt["se"])
        return b.cnt["se"]

    se_wk = stream_weight_1024(wkt, wkt_p0b, [("se", se_ln1b)])
    se_wv = stream_weight_1024(wvt, wvt_p0b, [("se", se_wk)],
                               eta_scale=True)

    wsP = mk_ws(112, "B2")
    assert wsP["end_kb"] <= 176
    planP_A, planP_B = mk_ttt(wsP)

    xpr = xp_s.rearrange("(g p) d -> g p d", p=128)
    pe_h1tr = {}
    ve_h1 = {}
    for sg in range(4):
        for gg in range(4):
            g = sg * 4 + gg
            slot = g % 2
            sxs = "sx" + str(slot)
            wex = ((("ve", ve_h1[g - 2]), (sxs, b.cnt[sxs]))
                   if g >= 2 else
                   (("se", se_ln1b), ("ve", ve_p0), (sxs, b.cnt[sxs])))
            b.dma("sync", ar(40 + 4 * slot, 4096, F32), xpr[g], sxs,
                  wait=wex)
            ldg = b.cnt[sxs]
            xg = ar(40 + 4 * slot, 4096, F32)
            vd = emit_ln_group(xg, ln_sum[:, 0:1], ln_sum[:, 1:2],
                               ((sxs, ldg),), ar(48, 4096, F32),
                               extra=(("ve", ve_h1.get(g - 1, 0)),))
            xnp = ar(48, 4096, F32)
            h1tok = ar(52 + 2 * slot, 2048, BF16)
            wtr = (("pe", pe_h1tr[g - 2]),) if g >= 2 else ()
            vd2 = emit_ln_group(xnp, ln_sum[:, 0:1], ln_sum[:, 1:2],
                                (("ve", vd),), h1tok, extra=wtr)
            b.op("vector",
                 lambda e, h1tok=h1tok: e.tensor_tensor(
                     out=h1tok[:], in0=h1tok[:], in1=ln1wb[:], op=ALU.mult),
                 wait=(("ve", vd2), ("se", se_wv)), inc=(("ve", 1),))
            b.op("vector",
                 lambda e, h1tok=h1tok: e.tensor_tensor(
                     out=h1tok[:], in0=h1tok[:], in1=ln1bb[:], op=ALU.add),
                 wait=(("ve", b.cnt["ve"]),), inc=(("ve", 1),))
            ve_h1[g] = b.cnt["ve"]
            h1pf3 = ar(56, 8192, BF16).rearrange("p (c t) -> p c t", t=512)
            emit_tr_group(gg, h1tok, h1pf3, ve_h1[g], "B3",
                          q8(7).bitcast(BF16))
            pe_h1tr[g] = b.cnt["pe"]
        ve_h1sg = b.cnt["ve"]
        h1pf = ar(56, 8192, BF16)
        kpf = ar(64, 8192, BF16)
        vtp = ar(72, 8192, BF16)
        for p in range(8):
            k = "A" + str(p % 4)
            for c in range(8):
                b.op("tensor",
                     lambda e, p=p, c=c, k=k: e.matmul(
                         bq(p % 4), wkt_p0b[:, c * 1024 + p * 128:
                                            c * 1024 + (p + 1) * 128],
                         h1pf[:, c * 512:(c + 1) * 512],
                         start=(c == 0), stop=(c == 7)),
                     wait=(bwait(k) + ((("ve", ve_h1sg), ("se", se_wk))
                                       if p == 0 else ())) if c == 0 else (),
                     inc=(("pe", 1),) if c == 7 else ())
            pv = b.cnt["pe"]
            b.op("vector",
                 lambda e, p=p: e.tensor_copy(
                     out=kpf[:, p * 512:(p + 1) * 512], in_=bq(p % 4)),
                 wait=(("pe", pv),), inc=(("ve", 1),))
            bdone(k, "ve")
        ve_kpf = b.cnt["ve"]
        for gg in range(4):
            k4, k5 = "B0", "B1"
            for c in range(8):
                for half in range(2):
                    kk = k4 if half == 0 else k5
                    bank = q8(4) if half == 0 else q8(5)
                    b.op("tensor",
                         lambda e, gg=gg, c=c, half=half, bank=bank:
                         e.matmul(
                             bank[:],
                             h1pf[:, c * 512 + gg * 128:
                                  c * 512 + (gg + 1) * 128],
                             wvt_p0b[:, c * 1024 + half * 512:
                                     c * 1024 + (half + 1) * 512],
                             start=(c == 0), stop=(c == 7)),
                         wait=(bwait(kk) + ((("ve", ve_kpf),)
                                            if gg == 0 and c == 0 else ())
                               ) if c == 0 else (),
                         inc=(("pe", 1),) if c == 7 and half == 1 else ())
            pv = b.cnt["pe"]
            b.op("vector",
                 lambda e, gg=gg: e.tensor_copy(
                     out=vtp[:, gg * 1024:gg * 1024 + 512], in_=q8(4)),
                 wait=(("pe", pv),), inc=(("ve", 1),))
            b.op("vector",
                 lambda e, gg=gg: e.tensor_copy(
                     out=vtp[:, gg * 1024 + 512:(gg + 1) * 1024],
                     in_=q8(5)),
                 wait=(("ve", b.cnt["ve"]),), inc=(("ve", 1),))
            bdone(k4, "ve"), bdone(k5, "ve")
        ve_vtp = b.cnt["ve"]
        if TRUNC == "p0b_proj":
            break
        wsP["kf_ready"] = (("ve", ve_kpf),)
        wsP["vt_ready"] = (("ve", ve_vtp),)
        wsP["w_ready"] = ()

        def KfP(gg, kpf=kpf):
            return lambda p, gg=gg, kpf=kpf: kpf[
                :, p * 512 + gg * 128:p * 512 + (gg + 1) * 128]

        nch = 1 if TRUNC == "p0b_c1" else (4 if TRUNC == "p0b_c4" else 4)
        if TRUNC == "p0b_c1" and sg > 0:
            nch = 0
        run_pair([], planP_A(sg * 4, KfP(0), None, True) if nch else [])
        for gg in range(nch):
            Ath = (planP_A(sg * 4 + gg + 1, KfP(gg + 1), None, True)
                   if gg < nch - 1 else [])
            run_pair(planP_B(sg * 4 + gg, KfP(gg),
                             vtp[:, gg * 1024:(gg + 1) * 1024],
                             None, None, True), Ath)
    b.op("vector", lambda e: e.tensor_scalar_mul(out=wst[:], in0=wst[:],
                                                 scalar1=flag_cell[:]),
         wait=(("ve", wsP["w_last_ve"][0]),), inc=(("ve", 1),))
    b.op("vector", lambda e: e.tensor_copy(out=wst_bf[:], in_=wst[:]),
         wait=(("ve", b.cnt["ve"]),), inc=(("ve", 1),))
    ve_p0b = b.cnt["ve"]
    ve_p0b_end2 = ve_p0b
    if "wmid" in DEBUG:
        b.op("scalar", lambda e: e.copy(out=dbg_stage[:, 0:512],
                                        in_=wst[:]),
             wait=(("ve", ve_p0b), ("sg", b.cnt["sg"])), inc=(("se", 1),))
        b.dma("gpsimd", dbg["wmid"][:, 0:512], dbg_stage[:, 0:512], "sg",
              wait=(("se", b.cnt["se"]),))
        b.op("gpsimd", None, wait=(("sg", b.cnt["sg"]),))


    if TRUNC == "p0b":
        b.flush()
        return nc
    Y = ar(32, 65536, F32)
    Y3 = Y.rearrange("p (g d) -> p g d", d=D)
    w13bf = [ar(96, 16384, BF16), ar(112, 16384, BF16)]
    w2bf = [ar(128, 8192, BF16), ar(136, 8192, BF16)]
    hhat = ar(144, 16384, BF16)
    combTb = ar(160, 4096, BF16)
    cbc_sb = ar(164, 4096, BF16)
    scoresE = ar(168, 8192, F32)
    sc_tok = ar(176, 1024, F32)
    bw_tok = ar(177, 1024, F32)
    sel_tok = ar(178, 1024, F32)
    aux_tok = ar(179, 256, F32)
    bw_hit = ar(179.25, 1024, F32)
    hs1 = ar(180, 2048, BF16)
    t1s = ar(182, 2048, BF16)
    combrow_t = ar(176, 4096, BF16)

    ve_p0b_end = b.cnt["ve"]
    b.op("vector", lambda e: e.memset(Y[:], 0.0),
         wait=(("ve", ve_p0b_end),), inc=(("ve", 1),))
    ve_y0 = b.cnt["ve"]

    for tb in range(4):
        for c in range(8):
            b.op("tensor",
                 lambda e, tb=tb, c=c: e.matmul(
                     q8(tb)[0:16, :], gwt_bf[:, c * 16:(c + 1) * 16],
                     xnf3[:, c, tb * 512:(tb + 1) * 512],
                     start=(c == 0), stop=(c == 7)),
                 wait=(bwait(K8[tb]) + (("ve", ve_p0),)) if c == 0 else (),
                 inc=(("pe", 1),) if c == 7 else ())
        pv = b.cnt["pe"]
        b.op("scalar",
             lambda e, tb=tb: e.activation(
                 out=scoresE[0:16, tb * 512:(tb + 1) * 512],
                 in_=q8(tb)[0:16, :], func=AF.Sigmoid),
             wait=(("pe", pv),), inc=(("se", 1),))
        bdone(K8[tb], "se")
    se_scores = b.cnt["se"]
    tb_ps = q8(4)
    for g in range(NG):
        b.op("tensor",
             lambda e, g=g: e.matmul(
                 tb_ps[:, g * 16:(g + 1) * 16],
                 scoresE[0:16, g * 128:(g + 1) * 128],
                 identf[0:16, 0:16], is_transpose=True,
                 start=True, stop=True),
             wait=(bwait("B0") + (("se", se_scores),)) if g == 0 else (),
             inc=(("pe", 1),) if g == NG - 1 else ())
    pv = b.cnt["pe"]
    b.op("vector", lambda e: e.tensor_copy(out=sc_tok[:, 0:256],
                                           in_=tb_ps[:, 0:256]),
         wait=(("pe", pv),), inc=(("ve", 1),))
    bdone("B0", "ve")
    ve_sct = b.cnt["ve"]

    def v3(t, w=16):
        return t[:, 0:256].rearrange("p (g q) -> p g q", q=w)

    b.op("vector",
         lambda e: e.tensor_tensor(
             out=v3(bw_tok), in0=v3(sc_tok),
             in1=gateb_b[:, 0:16].unsqueeze(1).broadcast_to((128, 16, 16)),
             op=ALU.add),
         wait=(("ve", ve_sct),), inc=(("ve", 1),))
    b.op("vector", lambda e: e.memset(sel_tok[:, 0:256], 0.0),
         wait=(("ve", b.cnt["ve"]),), inc=(("ve", 1),))
    for it in range(4):
        b.op("vector",
             lambda e: e.tensor_reduce(out=aux_tok[:, 0:16], in_=v3(bw_tok),
                                       axis=AX.X, op=ALU.max),
             wait=(("ve", b.cnt["ve"]),), inc=(("ve", 1),))
        b.op("vector",
             lambda e: e.tensor_tensor(
                 out=v3(bw_hit), in0=v3(bw_tok),
                 in1=aux_tok[:, 0:16].unsqueeze(2).broadcast_to(
                     (128, 16, 16)),
                 op=ALU.is_ge),
             wait=(("ve", b.cnt["ve"]),), inc=(("ve", 1),))
        b.op("vector",
             lambda e: e.tensor_max(out=sel_tok[:, 0:256],
                                    in0=sel_tok[:, 0:256],
                                    in1=bw_hit[:, 0:256]),
             wait=(("ve", b.cnt["ve"]),), inc=(("ve", 1),))
        b.op("vector",
             lambda e: e.tensor_scalar(out=bw_hit[:, 0:256],
                                       in0=bw_hit[:, 0:256],
                                       scalar1=1e30, scalar2=None,
                                       op0=ALU.mult),
             wait=(("ve", b.cnt["ve"]),), inc=(("ve", 1),))
        b.op("vector",
             lambda e: e.tensor_sub(out=bw_tok[:, 0:256],
                                    in0=bw_tok[:, 0:256],
                                    in1=bw_hit[:, 0:256]),
             wait=(("ve", b.cnt["ve"]),), inc=(("ve", 1),))
    b.op("vector",
         lambda e: e.tensor_mul(out=bw_tok[:, 0:256], in0=sel_tok[:, 0:256],
                                in1=sc_tok[:, 0:256]),
         wait=(("ve", b.cnt["ve"]),), inc=(("ve", 1),))
    b.op("vector",
         lambda e: e.tensor_reduce(out=aux_tok[:, 16:32], in_=v3(bw_tok),
                                   axis=AX.X, op=ALU.add),
         wait=(("ve", b.cnt["ve"]),), inc=(("ve", 1),))
    b.op("vector",
         lambda e: e.reciprocal(out=aux_tok[:, 32:48],
                                in_=aux_tok[:, 16:32]),
         wait=(("ve", b.cnt["ve"]),), inc=(("ve", 1),))
    b.op("vector",
         lambda e: e.tensor_scalar_mul(out=aux_tok[:, 32:48],
                                       in0=aux_tok[:, 32:48], scalar1=2.5),
         wait=(("ve", b.cnt["ve"]),), inc=(("ve", 1),))
    b.op("vector",
         lambda e: e.tensor_tensor(
             out=v3(bw_tok), in0=v3(bw_tok),
             in1=aux_tok[:, 32:48].unsqueeze(2).broadcast_to((128, 16, 16)),
             op=ALU.mult),
         wait=(("ve", b.cnt["ve"]),), inc=(("ve", 1),))
    ve_comb = b.cnt["ve"]
    ct_ps = pB[0:16, :]
    for g in range(NG):
        b.op("tensor",
             lambda e, g=g: e.matmul(
                 ct_ps[:, g * 128:(g + 1) * 128],
                 bw_tok[:, g * 16:(g + 1) * 16], identf[:],
                 is_transpose=True, start=True, stop=True),
             wait=(bwait("B0", "B1", "B2", "B3") + (("ve", ve_comb),)
                   ) if g == 0 else (),
             inc=(("pe", 1),) if g == NG - 1 else ())
    pv = b.cnt["pe"]
    b.op("vector", lambda e: e.tensor_copy(out=combTb[0:16, :],
                                           in_=ct_ps[:]),
         wait=(("pe", pv),), inc=(("ve", 1),))
    for k in ("B0", "B1", "B2", "B3"):
        bdone(k, "ve")
    ve_combT = b.cnt["ve"]

    se_stream_prev = b.cnt["se"]
    pe_w13_used = {}
    pe_w2_used = {}
    pe_cbc_prev = 0
    ve_cbc = 0
    last_Y_ve = {g: ve_y0 for g in range(NG)}
    ve_hhat_done = 0

    def stream_w2e(idx, dst_bf, wait0):
        wr = w2et[idx].rearrange("(c p) o -> c p o", p=128)
        ses = []
        for c in range(4):
            slot = c % 2
            ws_ = "n" + str(slot)
            wv = [("se", ses[c - 2])] if c >= 2 else []
            if c >= 2:
                wv.append((ws_, b.cnt[ws_]))
            b.dma("sync", wstage[:, slot * 1024:(slot + 1) * 1024], wr[c],
                  ws_, wait=tuple(wv) + (tuple(wait0) if c <= 1 else ()))
            ld = b.cnt[ws_]
            b.op("scalar",
                 lambda e, c=c, slot=slot: e.copy(
                     out=dst_bf[:, c * 1024:(c + 1) * 1024],
                     in_=wstage[:, slot * 1024:(slot + 1) * 1024]),
                 wait=((ws_, ld),), inc=(("se", 1),))
            ses.append(b.cnt["se"])
        return b.cnt["se"]

    for e_i in range(NE):
        sl_ = e_i % 2
        gate_w13 = [("se", se_stream_prev)]
        if e_i >= 2:
            gate_w13.append(("pe", pe_w13_used[e_i - 2]))
        se_a = stream_weight_1024(w13t[e_i], w13bf[sl_], gate_w13)
        gate_w2 = [("se", se_a)]
        if e_i >= 2:
            gate_w2.append(("pe", pe_w2_used[e_i - 2]))
        se_stream_prev = stream_w2e(e_i, w2bf[sl_], gate_w2)
        se_cast = se_stream_prev
        if e_i < NE - 1:
            b.dma("gpsimd", combrow_t[0:1, :], combTb[e_i:e_i + 1, :], "sg",
                  wait=(("ve", ve_combT), ("pe", pe_cbc_prev)))
            sgc = b.cnt["sg"]
            for qq in range(4):
                b.op("tensor",
                     lambda e, qq=qq: e.matmul(
                         q8(qq), ones_bf[:],
                         combrow_t[0:1, qq * 512:(qq + 1) * 512],
                         start=True, stop=True),
                     wait=((("sg", sgc),) + bwait(K8[0], K8[1], K8[2],
                                                  K8[3])) if qq == 0 else (),
                     inc=(("pe", 1),) if qq == 3 else ())
            pvc = b.cnt["pe"]
            b.op("vector",
                 lambda e: e.tensor_copy(out=cbc_sb[:, 0:2048],
                                         in_=pA[:, 0:2048]),
                 wait=(("pe", pvc), ("po", b.cnt["po"])) +
                 ((("ve", ve_cbc),) if ve_cbc else ()),
                 inc=(("ve", 1),))
            for kk in (K8[0], K8[1], K8[2], K8[3]):
                bdone(kk, "ve")
            pe_cbc_prev = b.cnt["pe"]
            ve_cbc = b.cnt["ve"]
        for pp in range(4):
            for ph in range(2):
                bk0 = 4 * ph
                for c in range(8):
                    for ti in range(2):
                        oc = pp + 4 * ti
                        for ts in range(2):
                            bi = bk0 + 2 * ti + ts
                            wt = ()
                            if c == 0:
                                wt = bwait(K8[bi])
                                if pp == 0 and ph == 0 and ti == 0                                         and ts == 0:
                                    wt = wt + (("se", se_cast),
                                               ("ve", ve_cbc),
                                               ("ve", ve_y0))
                            b.op("tensor",
                                 lambda e, c=c, oc=oc, ph=ph, ts=ts, bi=bi,
                                 sl_=sl_: e.matmul(
                                     q8(bi),
                                     w13bf[sl_][:, c * 1024 + oc * 128:
                                                c * 1024 + (oc + 1) * 128],
                                     xnf3[:, c, (2 * ph + ts) * 512:
                                          (2 * ph + ts + 1) * 512],
                                     start=(c == 0), stop=(c == 7)),
                                 wait=wt,
                                 inc=(("pe", 1),) if (c == 7 and ti == 1
                                                      and ts == 1) else ())
                pv = b.cnt["pe"]
                for ts in range(2):
                    gb, ub = bk0 + ts, bk0 + 2 + ts
                    col = (2 * ph + ts) * 512
                    b.op("scalar",
                         lambda e, gb=gb, ts=ts: e.activation(
                             out=hs1[:, ts * 512:(ts + 1) * 512],
                             in_=q8(gb), func=AF.Silu),
                         wait=(("pe", pv), ("po", b.cnt["po"]),
                               ("ve", b.cnt["ve"])),
                         inc=(("se", 1),))
                    sev = b.cnt["se"]
                    bdone(K8[gb], "se")
                    if e_i < NE - 1:
                        b.op("gpsimd",
                             lambda e, ts=ts, col=col: e.tensor_tensor(
                                 out=t1s[:, ts * 512:(ts + 1) * 512],
                                 in0=hs1[:, ts * 512:(ts + 1) * 512],
                                 in1=cbc_sb[:, col:col + 512],
                                 op=ALU.mult),
                             wait=(("se", sev), ("ve", b.cnt["ve"])),
                             inc=(("po", 1),))
                        pov = b.cnt["po"]
                        src_t = t1s
                        wt2 = (("po", pov),)
                    else:
                        src_t = hs1
                        wt2 = (("se", sev),)
                    b.op("vector",
                         lambda e, ub=ub, ts=ts, col=col, src_t=src_t,
                         pp=pp:
                         e.tensor_tensor(
                             out=hhat[:, pp * 2048 + col:
                                      pp * 2048 + col + 512],
                             in0=src_t[:, ts * 512:(ts + 1) * 512],
                             in1=q8(ub), op=ALU.mult),
                         wait=wt2 + (("ve", b.cnt["ve"]),),
                         inc=(("ve", 1),))
                    bdone(K8[ub], "ve")
        pe_w13_used[e_i] = b.cnt["pe"]
        ve_hhat_done = b.cnt["ve"]
        po_hhat = b.cnt["po"]
        se_hhat = b.cnt["se"]
        if TRUNC == "p1g" and e_i == 0:
            if "hh" in DEBUG:
                for i in range(8):
                    b.op("scalar",
                         lambda e, i=i: e.copy(
                             out=dbg_stage[:, 0:1024],
                             in_=hhat[:, i * 1024:(i + 1) * 1024]),
                         wait=(("ve", ve_hhat_done), ("po", po_hhat),
                               ("se", se_hhat), ("sg", b.cnt["sg"])),
                         inc=(("se", 1),))
                    b.dma("gpsimd", dbg["hh"][:, i * 1024:(i + 1) * 1024],
                          dbg_stage[:, 0:1024], "sg",
                          wait=(("se", b.cnt["se"]),))
                b.op("gpsimd", None, wait=(("sg", b.cnt["sg"]),))
            b.flush()
            return nc
        for g in range(NG):
            j = g % 4
            pt = pA if j < 2 else pB
            pv_view = pt[:, (j % 2) * 1024:(j % 2) * 1024 + 1024]
            kks = (K8[2 * j], K8[2 * j + 1])
            for ci in range(4):
                for half in range(2):
                    wt = ()
                    if ci == 0 and half == 0:
                        wt = bwait(*kks)
                        if g == 0:
                            wt = wt + (("ve", ve_hhat_done),
                                       ("po", po_hhat), ("se", se_hhat))
                    b.op("tensor",
                         lambda e, ci=ci, half=half, g=g, pv_view=pv_view,
                         sl_=sl_: e.matmul(
                             pv_view[:, half * 512:(half + 1) * 512],
                             hhat[:, ci * 2048 + g * 128:
                                  ci * 2048 + (g + 1) * 128],
                             w2bf[sl_][:, ci * 1024 + half * 512:
                                       ci * 1024 + (half + 1) * 512],
                             start=(ci == 0), stop=(ci == 3)),
                         wait=wt,
                         inc=(("pe", 1),) if (ci == 3 and half == 1) else ())
            pv = b.cnt["pe"]
            b.op("vector",
                 lambda e, g=g, pv_view=pv_view: e.tensor_tensor(
                     out=Y3[:, g, :], in0=Y3[:, g, :], in1=pv_view,
                     op=ALU.add),
                 wait=(("pe", pv), ("ve", last_Y_ve[g])), inc=(("ve", 1),))
            last_Y_ve[g] = b.cnt["ve"]
            bdone(kks[0], "ve"), bdone(kks[1], "ve")
        pe_w2_used[e_i] = b.cnt["pe"]

    g_tok = ar(96, 32768, BF16)
    g3 = g_tok.rearrange("p (g d) -> p g d", d=D)
    for g in range(NG):
        b.op("scalar",
             lambda e, g=g: e.activation(out=g3[:, g, :], in_=Y3[:, g, :],
                                         func=AF.Silu),
             wait=(("ve", last_Y_ve[g]), ("pe", pe_w13_used[NE - 1])),
             inc=(("se", 1),))
    se_g = b.cnt["se"]
    if "g" in DEBUG:
        for i in range(16):
            b.op("scalar",
                 lambda e, i=i: e.copy(out=dbg_stage[:, 0:1024],
                                       in_=g3[:, i, :]),
                 wait=(("se", se_g), ("sg", b.cnt["sg"])), inc=(("se", 1),))
            b.dma("gpsimd", dbg["g"][:, i * 1024:(i + 1) * 1024],
                  dbg_stage[:, 0:1024], "sg", wait=(("se", b.cnt["se"]),))
        b.op("gpsimd", None, wait=(("sg", b.cnt["sg"]),))
    if TRUNC == "p1":
        b.flush()
        return nc

    kf_t = ar(32, 32768, BF16)
    vt_t = ar(64, 32768, BF16)
    wq_bf = ar(128, 16384, BF16)
    wk_bf = ar(144, 16384, BF16)
    wv_bf = ar(160, 16384, BF16)
    h1sg = ar(176, 8192, BF16)
    sqst = ar(184, 1024, BF16)

    pe_p1_end = b.cnt["pe"]
    gate0 = [("se", se_stream_prev), ("pe", pe_p1_end), ("se", se_g)]
    se_q = stream_weight_1024(wqt, wq_bf, gate0)
    se_k = stream_weight_1024(wkt, wk_bf, [("se", se_q)])
    se_v = stream_weight_1024(wvt, wv_bf, [("se", se_k)], eta_scale=True)

    ve_q_done = {}
    for sg in range(4):
        t0 = sg * 512
        for c in range(8):
            b.op("tensor",
                 lambda e, c=c, t0=t0: e.matmul(
                     q8(0)[0:1, :], onesc_bf[:], xnf3[:, c, t0:t0 + 512],
                     start=(c == 0), stop=(c == 7)),
                 wait=(bwait("A0") + ((("se", se_v), ("ve", ve_p0b_end))
                                     if sg == 0 and c == 0 else ())
                       ) if c == 0 else (),
                 inc=(("pe", 1),) if c == 7 else ())
        pe_sum = b.cnt["pe"]
        for c in range(8):
            b.op("scalar",
                 lambda e, c=c, t0=t0: e.activation(
                     out=sqst[:, 0:512], in_=xnf3[:, c, t0:t0 + 512],
                     func=AF.Square),
                 wait=(("se", b.cnt["se"]), ("pe", b.cnt["pe"])),
                 inc=(("se", 1),))
            sev = b.cnt["se"]
            b.op("tensor",
                 lambda e, c=c: e.matmul(
                     q8(1)[0:1, :], onesc_bf[:], sqst[:, 0:512],
                     start=(c == 0), stop=(c == 7)),
                 wait=(("se", sev),) + (bwait("A1") if c == 0 else ()),
                 inc=(("pe", 1),))
        pe_sq = b.cnt["pe"]
        mu_r = wstage[0:1, 0:512]
        m2_r = wstage[0:1, 512:1024]
        var_r = wstage[0:1, 1024:1536]
        rstd_r = wstage[0:1, 1536:2048]
        b.op("vector", lambda e: e.tensor_scalar_mul(
            out=mu_r, in0=q8(0)[0:1, :], scalar1=1.0 / D),
             wait=(("pe", pe_sum), ("se", se_v)), inc=(("ve", 1),))
        bdone("A0", "ve")
        b.op("vector", lambda e: e.tensor_scalar_mul(
            out=m2_r, in0=q8(1)[0:1, :], scalar1=1.0 / D),
             wait=(("pe", pe_sq), ("ve", b.cnt["ve"])), inc=(("ve", 1),))
        bdone("A1", "ve")
        b.op("vector", lambda e: e.tensor_mul(out=var_r, in0=mu_r,
                                              in1=mu_r),
             wait=(("ve", b.cnt["ve"]),), inc=(("ve", 1),))
        b.op("vector", lambda e: e.tensor_sub(out=var_r, in0=m2_r,
                                              in1=var_r),
             wait=(("ve", b.cnt["ve"]),), inc=(("ve", 1),))
        b.op("scalar", lambda e: e.activation(out=var_r, in_=var_r,
                                              func=AF.Sqrt,
                                              bias=eps_cell[0:1, :]),
             wait=(("ve", b.cnt["ve"]),), inc=(("se", 1),))
        b.op("vector", lambda e: e.reciprocal(out=rstd_r, in_=var_r),
             wait=(("se", b.cnt["se"]),), inc=(("ve", 1),))
        ve_row = b.cnt["ve"]
        b.op("tensor", lambda e: e.matmul(q8(0), ones_f[:], mu_r,
                                          start=True, stop=True),
             wait=(("ve", ve_row),) + bwait("A0"), inc=(("pe", 1),))
        b.op("tensor", lambda e: e.matmul(q8(1), ones_f[:], rstd_r,
                                          start=True, stop=True),
             wait=bwait("A1"), inc=(("pe", 1),))
        pe_bc = b.cnt["pe"]
        ve_ap = b.cnt["ve"]
        for c in range(8):
            b.op("vector",
                 lambda e, c=c, t0=t0: e.tensor_sub(
                     out=sqst[:, 0:512].bitcast(F32)[:, 0:256] if False
                     else h1sg[:, c * 512:(c + 1) * 512],
                     in0=xnf3[:, c, t0:t0 + 512], in1=q8(0)),
                 wait=(("pe", pe_bc), ("ve", ve_ap)), inc=(("ve", 1),))
            b.op("vector",
                 lambda e, c=c: e.scalar_tensor_tensor(
                     out=h1sg[:, c * 512:(c + 1) * 512],
                     in0=h1sg[:, c * 512:(c + 1) * 512],
                     scalar=ln1w_cs[:, c:c + 1], in1=q8(1),
                     op0=ALU.mult, op1=ALU.mult),
                 wait=(("ve", b.cnt["ve"]),), inc=(("ve", 1),))
            b.op("vector",
                 lambda e, c=c: e.tensor_scalar_add(
                     out=h1sg[:, c * 512:(c + 1) * 512],
                     in0=h1sg[:, c * 512:(c + 1) * 512],
                     scalar1=ln1b_cs[:, c:c + 1]),
                 wait=(("ve", b.cnt["ve"]),), inc=(("ve", 1),))
            ve_ap = b.cnt["ve"]
        bdone("A0", "ve"), bdone("A1", "ve")
        ve_h1r = b.cnt["ve"]
        for which, wbf, dstt in (("q", wq_bf, None), ("k", wk_bf, kf_t)):
            for p in range(8):
                for c in range(8):
                    b.op("tensor",
                         lambda e, p=p, c=c, wbf=wbf: e.matmul(
                             q8(p), wbf[:, c * 1024 + p * 128:
                                        c * 1024 + (p + 1) * 128],
                             h1sg[:, c * 512:(c + 1) * 512],
                             start=(c == 0), stop=(c == 7)),
                         wait=(bwait(K8[p]) + ((("ve", ve_h1r),)
                                               if p == 0 else ())
                               ) if c == 0 else (),
                         inc=(("pe", 1),) if (p == 7 and c == 7) else ())
            pv = b.cnt["pe"]
            for half in range(2):
                t = pA if half == 0 else pB
                if dstt is None:
                    dv = xnf.rearrange("p (j t) -> p j t", t=NT)[
                        :, half * 4:half * 4 + 4, t0:t0 + 512]
                else:
                    dv = dstt.rearrange("p (j t) -> p j t", t=NT)[
                        :, half * 4:half * 4 + 4, t0:t0 + 512]
                b.op("vector",
                     lambda e, dv=dv, t=t: e.tensor_copy(out=dv,
                                                         in_=pstr(t, 512)),
                     wait=(("pe", pv),) if half == 0 else
                     (("ve", b.cnt["ve"]),),
                     inc=(("ve", 1),))
            for k in K8:
                bdone(k, "ve")
        for gg in range(4):
            jj = gg % 4
            pvt = pA if jj < 2 else pB
            view = pvt[:, (jj % 2) * 1024:(jj % 2) * 1024 + 1024]
            kks = (K8[2 * jj], K8[2 * jj + 1])
            for c in range(8):
                for half in range(2):
                    b.op("tensor",
                         lambda e, gg=gg, c=c, half=half, view=view, t0=t0:
                         e.matmul(
                             view[:, half * 512:(half + 1) * 512],
                             h1sg[:, c * 512 + gg * 128:
                                  c * 512 + (gg + 1) * 128],
                             wv_bf[:, c * 1024 + half * 512:
                                   c * 1024 + (half + 1) * 512],
                             start=(c == 0), stop=(c == 7)),
                         wait=(bwait(*kks)) if (c == 0 and half == 0)
                         else (),
                         inc=(("pe", 1),) if (c == 7 and half == 1) else ())
            pv = b.cnt["pe"]
            b.op("vector",
                 lambda e, gg=gg, view=view, sg=sg: e.tensor_copy(
                     out=vt_t[:, (sg * 4 + gg) * 1024:
                              (sg * 4 + gg + 1) * 1024], in_=view),
                 wait=(("pe", pv),), inc=(("ve", 1),))
            bdone(kks[0], "ve"), bdone(kks[1], "ve")
        ve_q_done[sg] = b.cnt["ve"]
    ve_proj = b.cnt["ve"]
    if TRUNC == "p2proj":
        if "proj" in DEBUG:
            for nm, tt, base in (("q", xnf, 0), ("k", kf_t, 4096),
                                 ("v", vt_t, 8192)):
                for i in range(4):
                    b.op("scalar",
                         lambda e, tt=tt, i=i: e.copy(
                             out=dbg_stage[:, 0:1024],
                             in_=tt[:, i * 1024:(i + 1) * 1024]),
                         wait=(("ve", ve_proj), ("sg", b.cnt["sg"])),
                         inc=(("se", 1),))
                    b.dma("gpsimd",
                          dbg["proj"][:, base + i * 1024:
                                      base + (i + 1) * 1024],
                          dbg_stage[:, 0:1024], "sg",
                          wait=(("se", b.cnt["se"]),))
            b.op("gpsimd", None, wait=(("sg", b.cnt["sg"]),))
        b.flush()
        return nc

    wsM = mk_ws(128, "B2")
    assert wsM["end_kb"] <= 184
    wsM["w_last_ve"] = [ve_p0b_end2]
    wsM["kf_ready"] = (("ve", ve_proj),)
    wsM["vt_ready"] = (("ve", ve_proj),)
    wsM["w_ready"] = ()
    planM_A, planM_B = mk_ttt(wsM)

    def KfM(c16):
        return lambda p, c16=c16: kf_t[:, p * 2048 + c16 * 128:
                                       p * 2048 + (c16 + 1) * 128]

    def QfM(c16):
        return lambda p, c16=c16: xnf[:, p * 2048 + c16 * 128:
                                      p * 2048 + (c16 + 1) * 128]

    def OfM(c16):
        def ofun(pv, half, t, c16=c16):
            dv = xnf.rearrange("p (j t) -> p j t", t=NT)[
                :, half * 4:half * 4 + 4, c16 * 128:(c16 + 1) * 128]
            b.op("vector",
                 lambda e, dv=dv, t=t: e.tensor_copy(
                     out=dv, in_=pstr(t, 128)),
                 wait=(("pe", pv),), inc=(("ve", 1),))
        return ofun

    nch2 = 1 if TRUNC == "p2c1" else 16
    run_pair([], planM_A(0, KfM(0), QfM(0), False))
    for c16 in range(nch2):
        Ath = (planM_A(c16 + 1, KfM(c16 + 1), QfM(c16 + 1), False)
               if c16 < nch2 - 1 else [])
        run_pair(planM_B(c16, KfM(c16),
                         vt_t[:, c16 * 1024:(c16 + 1) * 1024],
                         QfM(c16), OfM(c16), False), Ath)
    ve_ttt = b.cnt["ve"]
    if "oattn" in DEBUG:
        for i in range(16):
            b.op("scalar",
                 lambda e, i=i: e.copy(out=dbg_stage[:, 0:1024],
                                       in_=xnf[:, i * 1024:(i + 1) * 1024]),
                 wait=(("ve", ve_ttt), ("sg", b.cnt["sg"])),
                 inc=(("se", 1),))
            b.dma("gpsimd", dbg["oattn"][:, i * 1024:(i + 1) * 1024],
                  dbg_stage[:, 0:1024], "sg", wait=(("se", b.cnt["se"]),))
        b.op("gpsimd", None, wait=(("sg", b.cnt["sg"]),))
    if TRUNC in ("p2", "p2c1"):
        b.flush()
        return nc

    def xaV(g):
        return ar(4 * g, 4096, F32)

    ln2f_all = ar(64, 32768, BF16)
    xnfM = ar(128, 32768, BF16)
    h_ob = ar(128, 16384, BF16)
    w1blk = [ar(144, 8192, BF16), ar(152, 8192, BF16)]
    wot_bf = ar(160, 16384, BF16)

    def w2s(i):
        return ar(160 + 2 * i, 2048, BF16)

    def xldE(i):
        return ar(144 + 4 * i, 4096, F32)

    b2b3 = ar(176, 4096, F32)
    ln2g = ar(180, 2048, BF16)
    ln2bt = ar(182, 2048, BF16)
    ln2tok = ar(184, 2048, BF16)

    b.op("vector", lambda e: e.tensor_copy(out=xnfM[:], in_=xnf[:]),
         wait=(("ve", b.cnt["ve"]), ("pe", b.cnt["pe"]),
               ("se", b.cnt["se"]), ("po", b.cnt["po"])),
         inc=(("ve", 1),))
    ve_rep = b.cnt["ve"]

    b.dma("sync", wstage[0:1, 0:D], ln2w_row, "sc", wait=(("ve", ve_rep),))
    se_r = bc_mm(wstage[0:1, 0:D], ln2g, D, (("sc", b.cnt["sc"]),))
    b.dma("sync", wstage[0:1, 0:D], ln2b_row, "sc", wait=(("se", se_r),))
    se_r = bc_mm(wstage[0:1, 0:D], ln2bt, D, (("sc", b.cnt["sc"]),))
    b.dma("sync", wstage[0:1, 0:D], b2_row, "sc", wait=(("se", se_r),))
    se_r = bc_mm(wstage[0:1, 0:D], b2b3, D, (("sc", b.cnt["sc"]),))
    se_wo = stream_weight_1024(wot, wot_bf, [("se", se_r)])

    ep_sems = ["ex0", "ex1", "ex2", "ex3"]
    for g in range(NG):
        sxs = ep_sems[g % 4]
        wx = ((sxs, b.cnt[sxs]),)
        if g < 4:
            wx += (("ve", ve_rep),)
        b.dma("gpsimd", xaV(g), xr[g], sxs, wait=wx)
        ldx = b.cnt[sxs]
        jj = g % 2
        pvt = pA if jj == 0 else pB
        view = pvt[:, 0:1024]
        kks = (K8[4 * jj], K8[4 * jj + 1])
        for c in range(8):
            for half in range(2):
                b.op("tensor",
                     lambda e, c=c, half=half, g=g, view=view: e.matmul(
                         view[:, half * 512:(half + 1) * 512],
                         xnfM[:, c * 2048 + g * 128:
                              c * 2048 + (g + 1) * 128],
                         wot_bf[:, c * 1024 + half * 512:
                                c * 1024 + (half + 1) * 512],
                         start=(c == 0), stop=(c == 7)),
                     wait=(bwait(*kks) + ((("se", se_wo), ("ve", ve_rep))
                                         if g == 0 else ())
                           ) if c == 0 and half == 0 else (),
                     inc=(("pe", 1),) if (c == 7 and half == 1) else ())
        pv = b.cnt["pe"]
        b.op("vector",
             lambda e, g=g: e.tensor_scalar(
                 out=xaV(g), in0=xaV(g), scalar1=mu_cols[:, g:g + 1],
                 scalar2=rstd_cols[:, g:g + 1], op0=ALU.subtract,
                 op1=ALU.mult),
             wait=((sxs, ldx), ("ve", b.cnt["ve"])), inc=(("ve", 1),))
        b.op("vector",
             lambda e, g=g, view=view: e.tensor_add(
                 out=xaV(g), in0=xaV(g), in1=view),
             wait=(("ve", b.cnt["ve"]), ("pe", pv)), inc=(("ve", 1),))
        bdone(kks[0], "ve"), bdone(kks[1], "ve")
        ve_xa = b.cnt["ve"]
        b.op("vector", lambda e, g=g: e.reduce_sum(
            out=ln_sum[:, 0:1], in_=xaV(g), axis=AX.X),
             wait=(("ve", ve_xa),), inc=(("ve", 1),))
        b.op("scalar", lambda e, g=g: e.activation(
            out=ln2tok[:, 0:1024], in_=xaV(g), func=AF.Square,
            accum_out=ln_sum[:, 1:2]),
             wait=(("ve", ve_xa), ("se", b.cnt["se"]),
                   ("pe", b.cnt["pe"])), inc=(("se", 1),))
        vs, ss = b.cnt["ve"], b.cnt["se"]
        b.op("vector", lambda e: e.tensor_scalar_mul(
            out=ln_sum[:, 0:1], in0=ln_sum[:, 0:1], scalar1=1.0 / D),
             wait=(("ve", vs),), inc=(("ve", 1),))
        b.op("vector", lambda e: e.tensor_scalar_mul(
            out=ln_sum[:, 2:3], in0=ln_sum[:, 1:2], scalar1=1.0 / D),
             wait=(("se", ss), ("ve", b.cnt["ve"])), inc=(("ve", 1),))
        b.op("vector", lambda e: e.tensor_mul(
            out=ln_sum[:, 3:4], in0=ln_sum[:, 0:1], in1=ln_sum[:, 0:1]),
             wait=(("ve", b.cnt["ve"]),), inc=(("ve", 1),))
        b.op("vector", lambda e: e.tensor_sub(
            out=ln_sum[:, 2:3], in0=ln_sum[:, 2:3], in1=ln_sum[:, 3:4]),
             wait=(("ve", b.cnt["ve"]),), inc=(("ve", 1),))
        b.op("scalar", lambda e: e.activation(
            out=ln_sum[:, 3:4], in_=ln_sum[:, 2:3], func=AF.Sqrt,
            bias=eps_cell[:]),
             wait=(("ve", b.cnt["ve"]),), inc=(("se", 1),))
        b.op("vector", lambda e: e.reciprocal(
            out=ln_sum[:, 1:2], in_=ln_sum[:, 3:4]),
             wait=(("se", b.cnt["se"]),), inc=(("ve", 1),))
        b.op("vector",
             lambda e, g=g: e.tensor_scalar(
                 out=ln2tok[:, 0:1024], in0=xaV(g),
                 scalar1=ln_sum[:, 0:1], scalar2=ln_sum[:, 1:2],
                 op0=ALU.subtract, op1=ALU.mult),
             wait=(("ve", b.cnt["ve"]), ("se", b.cnt["se"])),
             inc=(("ve", 1),))
        b.op("vector",
             lambda e: e.tensor_mul(out=ln2tok[:, 0:1024],
                                    in0=ln2tok[:, 0:1024], in1=ln2g[:]),
             wait=(("ve", b.cnt["ve"]),), inc=(("ve", 1),))
        b.op("vector",
             lambda e: e.tensor_add(out=ln2tok[:, 0:1024],
                                    in0=ln2tok[:, 0:1024], in1=ln2bt[:]),
             wait=(("ve", b.cnt["ve"]),), inc=(("ve", 1),))
        ve_l2 = b.cnt["ve"]
        knb = q8(6).bitcast(BF16)
        for c in range(8):
            b.op("tensor",
                 lambda e, c=c: e.matmul(
                     knb[:, c * 128:(c + 1) * 128],
                     ln2tok[:, c * 128:(c + 1) * 128], identb[:],
                     is_transpose=True, start=True, stop=True),
                 wait=(bwait("B2") + (("ve", ve_l2),)) if c == 0 else (),
                 inc=(("pe", 1),) if c == 7 else ())
        pv = b.cnt["pe"]
        b.op("vector",
             lambda e, g=g: e.tensor_copy(
                 out=ln2f_all.rearrange("p (c t) -> p c t", t=NT)[
                     :, :, g * 128:(g + 1) * 128],
                 in_=knb.rearrange("p (c t) -> p c t", t=128)),
             wait=(("pe", pv),), inc=(("ve", 1),))
        bdone("B2", "ve")
    ve_l2_all = b.cnt["ve"]
    pe_woall = b.cnt["pe"]

    w1r = w1t.rearrange("(c p) o -> c p o", p=128)
    w2r = w2t.rearrange("(c p) o -> c p o", p=128)
    pe_w1_done = {}
    pe_w2_done = {}
    se_w1cast = {}
    se_w2cast = {}
    se_item = []
    it_idx = [0]

    def emit_stream(ob):
        for kind, idx in [("w1", c) for c in range(8)] + [
                ("w2", k) for k in range(4)]:
            it = it_idx[0]
            it_idx[0] += 1
            slot = it % 2
            ws_ = "w" + str(slot)
            wv = [("se", se_item[it - 2]) if it >= 2 else ("se", se_wo)]
            wv.append((ws_, b.cnt[ws_]))
            if kind == "w1":
                src = w1r[idx][:, ob * 512:(ob + 1) * 512]
                stg = wstage[:, slot * 1024:slot * 1024 + 512]
            else:
                src = w2r[ob * 4 + idx]
                stg = wstage[:, slot * 1024:(slot + 1) * 1024]
            b.dma("sync", stg, src, ws_, wait=tuple(wv))
            ld = b.cnt[ws_]
            cw = [(ws_, ld)]
            if ob < 2:
                cw.append(("pe", pe_woall))
            elif kind == "w1":
                cw.append(("pe", pe_w1_done[ob - 2]))
            else:
                cw.append(("pe", pe_w2_done[ob - 2]))
            if kind == "w1":
                b.op("scalar",
                     lambda e, idx=idx, ob=ob, stg=stg: e.copy(
                         out=w1blk[ob % 2][:, idx * 512:(idx + 1) * 512],
                         in_=stg),
                     wait=tuple(cw), inc=(("se", 1),))
            else:
                b.op("scalar",
                     lambda e, idx=idx, ob=ob, stg=stg: e.copy(
                         out=w2s((ob * 4 + idx) % 8)[:], in_=stg),
                     wait=tuple(cw), inc=(("se", 1),))
            se_item.append(b.cnt["se"])
            if kind == "w1" and idx == 7:
                se_w1cast[ob] = b.cnt["se"]
            if kind == "w2" and idx == 3:
                se_w2cast[ob] = b.cnt["se"]

    so_after = {}
    ep_ld = {}
    emit_stream(0)
    emit_stream(1)
    for ob in range(8):
        if ob >= 2:
            emit_stream(ob)
        for wv_ in range(2):
            for r in range(8):
                oc = wv_ * 2 + r // 4
                tq = r % 4
                for c in range(8):
                    wt = ()
                    if c == 0:
                        wt = bwait(K8[r])
                        if r == 0 and wv_ == 0:
                            wt += (("se", se_w1cast[ob]),)
                            if ob == 0:
                                wt += (("ve", ve_l2_all),)
                    b.op("tensor",
                         lambda e, c=c, oc=oc, tq=tq, r=r, ob=ob: e.matmul(
                             q8(r),
                             w1blk[ob % 2][:, c * 512 + oc * 128:
                                           c * 512 + oc * 128 + 128],
                             ln2f_all[:, c * 2048 + tq * 512:
                                      c * 2048 + (tq + 1) * 512],
                             start=(c == 0), stop=(c == 7)),
                         wait=wt, inc=(("pe", 1),) if c == 7 else ())
                pv = b.cnt["pe"]
                b.op("scalar",
                     lambda e, oc=oc, tq=tq, r=r, ob=ob: e.activation(
                         out=h_ob[:, oc * 2048 + tq * 512:
                                  oc * 2048 + (tq + 1) * 512],
                         in_=q8(r), func=AF.Gelu,
                         bias=b1_cs[:, ob * 4 + oc:ob * 4 + oc + 1]),
                     wait=(("pe", pv),), inc=(("se", 1),))
                bdone(K8[r], "se")
        pe_w1_done[ob] = b.cnt["pe"]
        se_h = b.cnt["se"]
        for v in range(4):
            for r in range(8):
                tt = v * 4 + r // 2
                hh = r % 2
                for k in range(4):
                    wt = ()
                    if k == 0:
                        wt = bwait(K8[r])
                        if r == 0 and v == 0:
                            wt += (("se", se_w2cast[ob]), ("se", se_h))
                    b.op("tensor",
                         lambda e, k=k, tt=tt, hh=hh, r=r, ob=ob: e.matmul(
                             q8(r),
                             h_ob[:, k * 2048 + tt * 128:
                                  k * 2048 + (tt + 1) * 128],
                             w2s((ob * 4 + k) % 8)[:, hh * 512:
                                                   (hh + 1) * 512],
                             start=(k == 0), stop=(k == 3)),
                         wait=wt, inc=(("pe", 1),) if k == 3 else ())
                pv = b.cnt["pe"]
                b.op("vector",
                     lambda e, tt=tt, hh=hh, r=r: e.tensor_add(
                         out=xaV(tt)[:, hh * 512:(hh + 1) * 512],
                         in0=xaV(tt)[:, hh * 512:(hh + 1) * 512],
                         in1=q8(r)),
                     wait=(("pe", pv),), inc=(("ve", 1),))
                bdone(K8[r], "ve")
            if ob == 7:
                ve_fin = b.cnt["ve"]
                for gg in range(4):
                    g = v * 4 + gg
                    slot = g % 4
                    sxs = ep_sems[slot]
                    wx = [(sxs, b.cnt[sxs])]
                    if g < 4:
                        wx += [("pe", pe_w1_done[7]), ("se", se_w1cast[7])]
                    else:
                        wx += [("eo" + str((g - 4) % 2), so_after[g - 4])]
                    b.dma("gpsimd", xldE(slot), xr[g], sxs, wait=tuple(wx))
                    ep_ld[g] = b.cnt[sxs]
                    b.op("vector",
                         lambda e, g=g: e.tensor_add(
                             out=xaV(g), in0=xaV(g), in1=b2b3[:]),
                         wait=(("ve", ve_fin),), inc=(("ve", 1),))
                    b.op("vector",
                         lambda e, g=g: e.tensor_mul(
                             out=xaV(g), in0=xaV(g), in1=g3[:, g, :]),
                         wait=(("ve", b.cnt["ve"]),), inc=(("ve", 1),))
                    b.op("vector",
                         lambda e, g=g, slot=slot: e.tensor_add(
                             out=xldE(slot)[:], in0=xldE(slot)[:],
                             in1=xaV(g)),
                         wait=(("ve", b.cnt["ve"]), (sxs, ep_ld[g])),
                         inc=(("ve", 1),))
                    ve_out = b.cnt["ve"]
                    sos = "eo" + str(g % 2)
                    b.dma("gpsimd", orr[g], xldE(slot)[:], sos,
                          wait=(("ve", ve_out), (sos, b.cnt[sos])))
                    so_after[g] = b.cnt[sos]
        pe_w2_done[ob] = b.cnt["pe"]
    b.op("sync", None, wait=(("eo0", b.cnt["eo0"]),
                             ("eo1", b.cnt["eo1"])))



    b.flush()
    return nc


def _prep_host(inputs):
    f = np.ascontiguousarray
    w = {}
    w["wqt"] = f(inputs["L0_attn_wq"].T)
    w["wkt"] = f(inputs["L0_attn_wk"].T)
    w["wvt"] = f(inputs["L0_attn_wv"].T)
    w["wot"] = f(inputs["L0_attn_wo"].T)
    w["w1t"] = f(inputs["L0_attn_w1"].T)
    w["w2t"] = f(inputs["L0_attn_w2"].T)
    w["w13t"] = f(np.concatenate(
        [inputs["L0_ffn_w13"].transpose(0, 2, 1),
         inputs["L0_ffn_w13_s"].T[None]], axis=0))
    w["w2et"] = f(np.concatenate(
        [inputs["L0_ffn_w2"].transpose(0, 2, 1),
         inputs["L0_ffn_w2_s"].T[None]], axis=0))
    w["gate_wt"] = f(inputs["L0_ffn_gate_w"].T)
    w["ln1w_c"] = f(inputs["L0_attn_ln1_w"].reshape(8, 128).T)
    w["ln1b_c"] = f(inputs["L0_attn_ln1_b"].reshape(8, 128).T)
    w["ln1w_row"] = f(inputs["L0_attn_ln1_w"].reshape(1, -1))
    w["ln1b_row"] = f(inputs["L0_attn_ln1_b"].reshape(1, -1))
    w["ln2w_row"] = f(inputs["L0_attn_ln2_w"].reshape(1, -1))
    w["ln2b_row"] = f(inputs["L0_attn_ln2_b"].reshape(1, -1))
    w["b1_c"] = f(inputs["L0_attn_b1"].reshape(32, 128).T)
    w["b2_row"] = f(inputs["L0_attn_b2"].reshape(1, -1))
    w["gateb_col"] = f(inputs["L0_ffn_gate_b"].reshape(16, 1))
    w["gateb_row"] = f(inputs["L0_ffn_gate_b"].reshape(1, 16))
    w["lr_row"] = f(np.repeat(inputs["L0_attn_lr_scale"].reshape(16), 64
                              ).reshape(1, -1))
    i_, j_ = np.indices((128, 128))
    bi, bj = i_ // 32, j_ // 32
    w["stril"] = (i_ > j_).astype(np.float32)
    w["bdl"] = ((i_ > j_) & (bi == bj)).astype(np.float32)
    w["bdu"] = ((i_ < j_) & (bi == bj)).astype(np.float32)
    w["offu"] = ((i_ < j_) & (bi != bj)).astype(np.float32)
    w["offl"] = ((i_ > j_) & (bi != bj)).astype(np.float32)
    w["striu"] = (i_ < j_).astype(np.float32)
    w["ident_bf"] = np.eye(128).astype(ml_dtypes.bfloat16)
    w["ident_f"] = np.eye(128).astype(np.float32)
    return w


def kernel(**inputs):
    inputs = {k: np.asarray(v) for k, v in inputs.items()}
    x = np.ascontiguousarray(inputs["L0_attn_x"].astype(np.float32,
                                                        copy=False))
    shared = _prep_host(inputs)
    nc = build_program()
    in_maps = []
    ones_col = np.ones((128, 1), np.float32)
    zeros_col = np.zeros((128, 1), np.float32)
    for c in range(8):
        bb, hh = c // 2, c % 2
        m = dict(shared)
        m["x_s"] = np.ascontiguousarray(x[bb, hh * NT:(hh + 1) * NT])
        m["xp_s"] = np.ascontiguousarray(x[bb, 0:NT])
        m["wflag"] = ones_col if hh == 1 else zeros_col
        in_maps.append(m)
    res = run_bass_kernel_spmd(nc, in_maps, core_ids=list(range(8)))
    out = np.empty((4, 4096, 1024), np.float32)
    for c in range(8):
        bb, hh = c // 2, c % 2
        out[bb, hh * NT:(hh + 1) * NT] = res.results[c]["out_s"]
    return out


# version 07
# speedup vs baseline: 1.0853x; 1.0853x over previous
import numpy as np
import ml_dtypes
import concourse.bass as bass
import concourse.mybir as mybir
from concourse.bass_utils import run_bass_kernel_spmd

F32 = mybir.dt.float32
BF16 = mybir.dt.bfloat16
AF = mybir.ActivationFunctionType
ALU = mybir.AluOpType
AX = mybir.AxisListType

NT = 2048
NG = 16
D = 1024
H = 16
DH = 64
NE = 17
INNER_LR = 0.01
EPS = 1e-5

DEBUG = []
CHUNK_STAGE = 99
TRUNC = None

ENGINES = ("sync", "tensor", "vector", "scalar", "gpsimd")


class B:
    def __init__(self, nc):
        self.nc = nc
        self.q = {e: [] for e in ENGINES}
        self.sem = {}
        self.cnt = {}

    def new_sem(self, name):
        self.sem[name] = self.nc.alloc_semaphore(name)
        self.cnt[name] = 0

    def op(self, eng, fn=None, wait=(), inc=()):
        for s, n in inc:
            self.cnt[s] += n
        if wait:
            best = {}
            for s, v in wait:
                if v > best.get(s, 0):
                    best[s] = v
            wait = tuple(best.items())
        self.q[eng].append((fn, tuple(wait), tuple(inc)))

    def dma(self, eng, out, in_, sem, wait=()):
        self.op(eng, lambda e, o=out, i=in_: e.dma_start(out=o, in_=i),
                wait=wait, inc=((sem, 16),))
        return self.cnt[sem]

    def flush(self):
        with self.nc.Block() as blk:
            for eng_name in ENGINES:
                ops = self.q[eng_name]
                if not ops:
                    continue

                def body(e, ops=ops):
                    for fn, waits, incs in ops:
                        for s, v in waits:
                            e.wait_ge(self.sem[s], v)
                        inst = fn(e) if fn is not None else None
                        if incs:
                            assert inst is not None
                            for s, n in incs:
                                inst.then_inc(self.sem[s], n)
                getattr(blk, eng_name)(body)


def cdiv(a, b):
    return (a + b - 1) // b


def build_program():
    nc = bass.Bass("TRN2", target_bir_lowering=False, debug=False)
    b = B(nc)
    for s in ("sc", "sx0", "sx1", "w0", "w1", "m0", "m1", "n0", "n1",
              "so0", "so1", "sg", "pe", "ve", "se", "po",
              "ex0", "ex1", "ex2", "ex3", "eo0", "eo1"):
        b.new_sem(s)

    di = lambda n, s: nc.dram_tensor(n, s, F32, kind="ExternalInput").ap()
    x_s = di("x_s", [NT, D])
    xp_s = di("xp_s", [NT, D])
    wqt = di("wqt", [D, D])
    wkt = di("wkt", [D, D])
    wvt = di("wvt", [D, D])
    wot = di("wot", [D, D])
    w1t = di("w1t", [D, 4 * D])
    w2t = di("w2t", [4 * D, D])
    w13t = di("w13t", [NE, D, D])
    w2et = di("w2et", [NE, 512, D])
    gate_wt = di("gate_wt", [D, 16])
    ln1w_c = di("ln1w_c", [128, 8])
    ln1b_c = di("ln1b_c", [128, 8])
    ln1w_row = di("ln1w_row", [1, D])
    ln1b_row = di("ln1b_row", [1, D])
    ln2w_row = di("ln2w_row", [1, D])
    ln2b_row = di("ln2b_row", [1, D])
    b1_c = di("b1_c", [128, 32])
    b2_row = di("b2_row", [1, D])
    gateb_col = di("gateb_col", [16, 1])
    gateb_row = di("gateb_row", [1, 16])
    lr_row = di("lr_row", [1, D])
    stril_in = di("stril", [128, 128])
    bdl_in = di("bdl", [128, 128])
    bdu_in = di("bdu", [128, 128])
    offu_in = di("offu", [128, 128])
    offl_in = di("offl", [128, 128])
    striu_in = di("striu", [128, 128])
    wflag = di("wflag", [128, 1])
    ident_in = nc.dram_tensor("ident_bf", [128, 128], BF16,
                              kind="ExternalInput").ap()
    identf_in = di("ident_f", [128, 128])
    out_s = nc.dram_tensor("out_s", [NT, D], F32, kind="ExternalOutput").ap()

    dbg = {}
    for name in DEBUG:
        dbg[name] = nc.dram_tensor("dbg_" + name, [128, 16384], F32,
                                   kind="ExternalOutput").ap()

    sb = nc.alloc_sbuf_tensor
    masks = sb("masks", [128, 6 * 128], F32)
    M_STRIL, M_BDL, M_BDU, M_OFFU, M_OFFL, M_STRIU = (0, 128, 256, 384,
                                                      512, 640)
    identb = sb("identb", [128, 128], BF16)
    identf = sb("identf", [128, 128], F32)
    eta16 = sb("eta16", [128, 16], F32)
    ln1w_cs = sb("ln1w_cs", [128, 8], F32)
    ln1b_cs = sb("ln1b_cs", [128, 8], F32)
    b1_cs = sb("b1_cs", [128, 32], F32)
    gwt_bf = sb("gwt_bf", [128, 8 * 16], BF16)
    gateb_cs = sb("gateb_cs", [16, 1], F32)
    gateb_b = sb("gateb_b", [128, 16], F32)
    mu_cols = sb("mu_cols", [128, NG], F32)
    rstd_cols = sb("rstd_cols", [128, NG], F32)
    wst = sb("wst", [128, 8 * DH], F32)
    wst_bf = sb("wst_bf", [128, 8 * DH], BF16)
    eps_cell = sb("eps_cell", [128, 1], F32)
    ones_bf = sb("ones_bf", [1, 128], BF16)
    ones_f = sb("ones_f", [1, 128], F32)
    onesc_bf = sb("onesc_bf", [128, 1], BF16)
    flag_cell = sb("flag_cell", [128, 1], F32)
    ln_sum = sb("ln_sum", [128, 4], F32)
    ln_row = sb("ln_row", [1, 8 * 4], F32)
    wstage = sb("wstage", [128, 2 * 1024], F32)

    AR = 186 * 1024
    arena = sb("arena", [128, AR], mybir.dt.uint8)

    def ar(off_kb, nbytes, dt):
        off = int(off_kb * 1024)
        assert off % 4 == 0 and off + nbytes <= AR, (off_kb, nbytes)
        return arena[:, off:off + nbytes].bitcast(dt)

    pA = nc.alloc_psum_tensor("pA", [128, 2048], F32)
    pB = nc.alloc_psum_tensor("pB", [128, 2048], F32)
    bank_evac = {}
    K8 = ["A0", "A1", "A2", "A3", "B0", "B1", "B2", "B3"]

    def q8(i):
        t = pA if i < 4 else pB
        return t[:, (i % 4) * 512:(i % 4) * 512 + 512]

    def bq(i):
        return q8(i)

    def pstr(t, w):
        return t[:].rearrange("p (j q) -> p j q", q=512)[:, :, 0:w]

    def bwait(*keys):
        return tuple(bank_evac[k] for k in keys if k in bank_evac)

    def bdone(k, sem):
        bank_evac[k] = (sem, b.cnt[sem])

    b.dma("sync", masks[:, 0:128], stril_in, "sc")
    b.dma("sync", masks[:, 128:256], bdl_in, "sc")
    b.dma("sync", masks[:, 256:384], bdu_in, "sc")
    b.dma("sync", masks[:, 384:512], offu_in, "sc")
    b.dma("sync", masks[:, 512:640], offl_in, "sc")
    b.dma("sync", masks[:, 640:768], striu_in, "sc")
    b.dma("sync", identb[:], ident_in, "sc")
    b.dma("sync", identf[:], identf_in, "sc")
    b.dma("sync", ln1w_cs[:], ln1w_c, "sc")
    b.dma("sync", ln1b_cs[:], ln1b_c, "sc")
    b.dma("sync", b1_cs[:], b1_c, "sc")
    b.dma("sync", gateb_cs[:], gateb_col, "sc")
    b.dma("sync", flag_cell[:], wflag, "sc")
    b.op("vector", lambda e: e.memset(eps_cell[:], EPS))
    b.op("vector", lambda e: e.memset(ones_bf[:], 1.0))
    b.op("vector", lambda e: e.memset(ones_f[:], 1.0))
    b.op("vector", lambda e: e.memset(onesc_bf[:], 1.0))
    b.op("vector", lambda e: e.memset(wst[:], 0.0))
    b.op("vector", lambda e: e.memset(wst_bf[:], 0.0), inc=(("ve", 1),))

    def bc_mm(src_ap, dst_tile, w, wait_first, dst_off=0):
        for hf in range(cdiv(w, 512)):
            ww = min(512, w - hf * 512)
            k = "A" + str(hf % 2)
            bk = bq(hf % 2)
            b.op("tensor",
                 lambda e, s=src_ap, hw=hf * 512, ww=ww, bk=bk:
                 e.matmul(bk[:, 0:ww], ones_f[:], s[0:1, hw:hw + ww],
                          start=True, stop=True),
                 wait=(wait_first if hf == 0 else ()) + bwait(k),
                 inc=(("pe", 1),))
            pv = b.cnt["pe"]
            b.op("scalar",
                 lambda e, d=dst_tile, hw=hf * 512, ww=ww, bk=bk, o=dst_off:
                 e.copy(out=d[:, o + hw:o + hw + ww], in_=bk[:, 0:ww]),
                 wait=(("pe", pv),), inc=(("se", 1),))
            bdone(k, "se")
        return b.cnt["se"]

    rowbuf = ar(0, 8192, F32)
    b.dma("sync", rowbuf[0:1, 0:D], lr_row, "sc")
    gwr = gate_wt.rearrange("(c p) n -> p c n", p=128)
    b.dma("sync", wstage[:, 0:128].rearrange("p (c n) -> p c n", n=16),
          gwr, "sc")
    b.dma("sync", rowbuf[0:1, D + 16:D + 32], gateb_row, "sc")
    sc_all = b.cnt["sc"]
    b.op("vector", lambda e: e.tensor_scalar_mul(out=rowbuf[0:1, 0:D],
                                                 in0=rowbuf[0:1, 0:D],
                                                 scalar1=INNER_LR),
         wait=(("sc", sc_all),), inc=(("ve", 1),))
    b.op("vector",
         lambda e: e.tensor_copy(
             out=rowbuf[0:1, D:D + 16].rearrange("p (h r) -> p h r", r=1),
             in_=rowbuf[0:1, 0:D].rearrange("p (h r) -> p h r", r=64)[
                 :, :, 0:1]),
         wait=(("ve", b.cnt["ve"]),), inc=(("ve", 1),))
    bc_mm(rowbuf[0:1, D:D + 16], eta16, 16, (("ve", b.cnt["ve"]),))
    bc_mm(rowbuf[0:1, D + 16:D + 32], gateb_b, 16,
          (("sc", sc_all),))
    b.op("scalar", lambda e: e.copy(out=gwt_bf[:], in_=wstage[:, 0:128]),
         wait=(("sc", sc_all),), inc=(("se", 1),))
    se_p0a = b.cnt["se"]

    def eta_rep(w):
        return eta16[:, 0:16].unsqueeze(2).broadcast_to((128, 16, w))

    def mask_rep(off, n_h=16):
        return masks[:, off:off + 128].unsqueeze(1).broadcast_to(
            (128, n_h, 128))

    xnf = ar(0, 32768, BF16)
    xnf3 = xnf.rearrange("p (c t) -> p c t", t=NT)

    def p0_stage(i):
        return ar(32 + 4 * i, 4096, F32)

    def p0_tok(i):
        return ar(40 + 2 * i, 2048, BF16)

    xr = x_s.rearrange("(g p) d -> g p d", p=128)
    orr = out_s.rearrange("(g p) d -> g p d", p=128)

    def emit_ln_group(xg, mu, rstd, wait_in, xn_out_bf, extra=()):
        b.op("vector", lambda e: e.reduce_sum(out=ln_sum[:, 0:1], in_=xg,
                                              axis=AX.X),
             wait=wait_in, inc=(("ve", 1),))
        b.op("scalar", lambda e: e.activation(out=xn_out_bf, in_=xg,
                                              func=AF.Square,
                                              accum_out=ln_sum[:, 1:2]),
             wait=wait_in + tuple(extra), inc=(("se", 1),))
        vs, ss = b.cnt["ve"], b.cnt["se"]
        w = xg.shape[-1]
        b.op("vector", lambda e: e.tensor_scalar_mul(out=mu,
                                                     in0=ln_sum[:, 0:1],
                                                     scalar1=1.0 / w),
             wait=(("ve", vs),), inc=(("ve", 1),))
        b.op("vector", lambda e: e.tensor_scalar_mul(out=ln_sum[:, 2:3],
                                                     in0=ln_sum[:, 1:2],
                                                     scalar1=1.0 / w),
             wait=(("se", ss), ("ve", b.cnt["ve"])), inc=(("ve", 1),))
        b.op("vector", lambda e: e.tensor_mul(out=ln_sum[:, 3:4], in0=mu,
                                              in1=mu),
             wait=(("ve", b.cnt["ve"]),), inc=(("ve", 1),))
        b.op("vector", lambda e: e.tensor_sub(out=ln_sum[:, 2:3],
                                              in0=ln_sum[:, 2:3],
                                              in1=ln_sum[:, 3:4]),
             wait=(("ve", b.cnt["ve"]),), inc=(("ve", 1),))
        b.op("scalar", lambda e: e.activation(out=ln_sum[:, 3:4],
                                              in_=ln_sum[:, 2:3],
                                              func=AF.Sqrt,
                                              bias=eps_cell[:]),
             wait=(("ve", b.cnt["ve"]),), inc=(("se", 1),))
        b.op("vector", lambda e: e.reciprocal(out=rstd, in_=ln_sum[:, 3:4]),
             wait=(("se", b.cnt["se"]),), inc=(("ve", 1),))
        b.op("vector", lambda e: e.tensor_scalar(out=xn_out_bf, in0=xg,
                                                 scalar1=mu, scalar2=rstd,
                                                 op0=ALU.subtract,
                                                 op1=ALU.mult),
             wait=(("ve", b.cnt["ve"]),) + tuple(extra), inc=(("ve", 1),))
        return b.cnt["ve"]

    def emit_tr_group(g, src_bf, dst3, ve_ready, key, bank_bf, extra=()):
        for c in range(8):
            b.op("tensor",
                 lambda e, c=c: e.matmul(
                     bank_bf[:, c * 128:(c + 1) * 128],
                     src_bf[:, c * 128:(c + 1) * 128], identb[:],
                     is_transpose=True, start=True, stop=True),
                 wait=(((("ve", ve_ready),) + tuple(extra) + bwait(key))
                       if c == 0 else ()),
                 inc=(("pe", 1),) if c == 7 else ())
        pev = b.cnt["pe"]
        b.op("vector",
             lambda e: e.tensor_copy(
                 out=dst3[:, :, g * 128:(g + 1) * 128],
                 in_=bank_bf[:].rearrange("p (c t) -> p c t", t=128)),
             wait=(("pe", pev),), inc=(("ve", 1),))
        bdone(key, "ve")
        return b.cnt["ve"]

    ve_ln_done, pe_tr = {}, {}
    for g in range(NG):
        sl_ = g % 2
        sxs = "sx" + str(sl_)
        wex = ((("ve", ve_ln_done[g - 2]), (sxs, b.cnt[sxs]))
               if g >= 2 else ())
        b.dma("sync", p0_stage(sl_), xr[g], sxs, wait=wex)
        ldg = b.cnt[sxs]
        xw = (("pe", pe_tr[g - 2]),) if g >= 2 else ()
        vd = emit_ln_group(p0_stage(sl_), mu_cols[:, g:g + 1],
                           rstd_cols[:, g:g + 1], ((sxs, ldg),),
                           p0_tok(sl_), extra=xw)
        ve_ln_done[g] = vd
        exw = (("se", se_p0a),) if g < 2 else ()
        emit_tr_group(g, p0_tok(sl_), xnf3, vd, "B" + str(sl_),
                      q8(4 + sl_).bitcast(BF16), extra=exw)
        pe_tr[g] = b.cnt["pe"]
    ve_p0 = b.cnt["ve"]

    dbg_stage = sb("dbg_stage", [128, 1024], F32) if DEBUG else None

    def dump(name, src_fn, n_cols, wait):
        for i in range(cdiv(n_cols, 1024)):
            b.op("scalar",
                 lambda e, i=i: e.copy(out=dbg_stage[:, 0:1024],
                                       in_=src_fn(i)),
                 wait=tuple(wait) + (("sg", b.cnt["sg"]),), inc=(("se", 1),))
            b.dma("gpsimd", dbg[name][:, i * 1024:(i + 1) * 1024],
                  dbg_stage[:, 0:1024], "sg", wait=(("se", b.cnt["se"]),))
        b.op("gpsimd", None, wait=(("sg", b.cnt["sg"]),))

    if "xnf" in DEBUG:
        dump("xnf", lambda i: xnf[:, i * 1024:(i + 1) * 1024], 16384,
             [("ve", ve_p0)])
    if TRUNC == "p0":
        b.flush()
        return nc

    def wslice(t, h):
        return t[(h % 2) * 64:(h % 2) * 64 + 64,
                 (h // 2) * 64:(h // 2) * 64 + 64]

    def mk_ttt(ws):
        a_, bb_, cc_, dd_ = ws["sa"], ws["sb"], ws["sc"], ws["sd"]
        ee_, ff_, gg_ = ws["se_"], ws["sf"], ws["sg_"]
        P = ws["ttblk"]
        evq = ws["evq"]
        rr = ws["rr"]

        def grp_banks(g):
            return ("A0", "A1", "A2", "A3") if g == 0 else (
                "B0", "B1", "B2", "B3")

        def grp_t(g):
            return pA if g == 0 else pB

        def qview(t, q, w, n=4):
            return t[:, q * 4 * w:(q * 4 + n) * w].rearrange(
                "p (j x) -> p j x", x=w)

        def dep(key, q):
            return (evq[(key, q)],) if (key, q) in evq else ()

        def depall(key):
            best = {}
            for q in range(4):
                if (key, q) in evq:
                    s, v = evq[(key, q)]
                    best[s] = max(best.get(s, 0), v)
            return tuple(best.items())

        def round4(dst_key, q, mm_fn, evac_fn, deps, w):
            g = rr[0] % 2
            rr[0] += 1
            for j in range(4):
                wt = ()
                if j == 0:
                    wt = bwait(*grp_banks(g)) + tuple(deps)
                mm_fn(j, q8(4 * g + j)[:, 0:w], wt, j == 3)
            evac_fn(g, q, w)

        EV_CYCLE = ("scalar", "vector", "scalar")
        EV_SEM = {"vector": "ve", "scalar": "se", "gpsimd": "po"}

        def ev_to(dst, dst_key, eng_alt=True):
            def f(g, q, w):
                eng = EV_CYCLE[ws["er"][0] % 3] if eng_alt else "vector"
                ws["er"][0] += 1
                sem = EV_SEM[eng]
                if eng == "scalar":
                    b.op("scalar",
                         lambda e, g=g, q=q, w=w: e.copy(
                             out=qview(dst, q, w), in_=pstr(grp_t(g), w)),
                         wait=(("pe", b.cnt["pe"]),),
                         inc=((sem, 1),))
                else:
                    b.op(eng,
                         lambda e, g=g, q=q, w=w: e.tensor_copy(
                             out=qview(dst, q, w), in_=pstr(grp_t(g), w)),
                         wait=(("pe", b.cnt["pe"]),), inc=((sem, 1),))
                evq[(dst_key, q)] = (sem, b.cnt[sem])
                for k in grp_banks(g):
                    bdone(k, sem)
            return f

        def hs4(t, q, j):
            return t[:, (4 * q + j) * 128:(4 * q + j) * 128 + 128]

        def hs(t, h):
            return t[:, h * 128:(h + 1) * 128]

        POST_R1 = 8
        POST_U = 28

        def plan_A(c, Kf, Qf, state_only):
            th = []
            post_o = POST_U if state_only else POST_U + 2
            kn_t = ws["kn"][c % 2]

            def kn_round(c=c, Kf=Kf, kn_t=kn_t):
                knb = ws["kn_psum"]
                for p in range(8):
                    b.op("tensor",
                         lambda e, p=p: e.matmul(
                             knb[:, p * 128:(p + 1) * 128], Kf(p),
                             identb[:], is_transpose=True, start=True,
                             stop=True),
                         wait=(bwait(ws["kn_key"]) + tuple(ws["kf_ready"])
                               ) if p == 0 else (),
                         inc=(("pe", 1),) if p == 7 else ())
                b.op("scalar",
                     lambda e: e.copy(out=kn_t[:], in_=knb[:]),
                     wait=(("pe", b.cnt["pe"]),), inc=(("se", 1),))
                bdone(ws["kn_key"], "se")
                ws["ve_kn"][c % 2] = ("se", b.cnt["se"])
            th.append((0, kn_round))

            for q in range(4):
                def gq(q=q, Kf=Kf):
                    def gmm(j, out, wt, last, q=q):
                        h = 4 * q + j
                        b.op("tensor",
                             lambda e, h=h, out=out: e.matmul(
                                 out, Kf2(Kf, h), Kf2(Kf, h), start=True,
                                 stop=True),
                             wait=wt, inc=(("pe", 1),) if last else ())

                    def gev(g, q, w):
                        b.op("vector",
                             lambda e, g=g, q=q, w=w: e.tensor_tensor(
                                 out=qview(a_, q, w), in0=pstr(grp_t(g), w),
                                 in1=eta16[:, 4 * q:4 * q + 4].unsqueeze(
                                     2).broadcast_to((128, 4, w)),
                                 op=ALU.mult),
                             wait=(("pe", b.cnt["pe"]),), inc=(("ve", 1),))
                        evq[("ng", q)] = ("ve", b.cnt["ve"])
                        for k in grp_banks(g):
                            bdone(k, "ve")
                        b.op("gpsimd",
                             lambda e, q=q: e.tensor_tensor(
                                 out=qview(bb_, q, 128),
                                 in0=qview(a_, q, 128),
                                 in1=mask_rep(M_BDL, 4), op=ALU.mult),
                             wait=(("ve", b.cnt["ve"]),
                                   ("po", b.cnt["po"])),
                             inc=(("po", 1),))
                        evq[("D", q)] = ("po", b.cnt["po"])
                        b.op("gpsimd",
                             lambda e, q=q: e.tensor_tensor(
                                 out=qview(cc_, q, 128),
                                 in0=qview(a_, q, 128),
                                 in1=mask_rep(M_BDU, 4), op=ALU.mult),
                             wait=(("po", b.cnt["po"]),), inc=(("po", 1),))
                        evq[("Dt", q)] = ("po", b.cnt["po"])
                        b.op("gpsimd",
                             lambda e, q=q: e.tensor_tensor(
                                 out=qview(dd_, q, 128),
                                 in0=qview(a_, q, 128),
                                 in1=mask_rep(M_OFFL, 4), op=ALU.mult),
                             wait=(("po", b.cnt["po"]),), inc=(("po", 1),))
                        evq[("Loff", q)] = ("po", b.cnt["po"])

                    round4("ng", q, gmm, gev, ws["kf_ready"], 128)
                th.append((0, gq))

            chain = [
                ("D2", ee_, "Dt", cc_, "D", bb_),
                ("D2t", ff_, "D", bb_, "Dt", cc_),
                ("D4", gg_, "D2t", ff_, "D2", ee_),
                ("D4t", cc_, "D2", ee_, "D2t", ff_),
                ("D8", ff_, "D4t", cc_, "D4", gg_),
                ("D8t", a_, "D4", gg_, "D4t", cc_),
                ("Dt16", cc_, "D8", ff_, "D8t", a_),
            ]
            for dk, dt_, lk, lt, rk, rt_ in chain:
                for q in range(4):
                    def cq(dk=dk, dt_=dt_, lk=lk, lt=lt, rk=rk, rt_=rt_,
                           q=q):
                        def smm(j, out, wt, last, lt=lt, rt_=rt_, q=q):
                            b.op("tensor",
                                 lambda e, j=j, out=out, lt=lt, rt_=rt_,
                                 q=q: e.matmul(
                                     out, hs4(lt, q, j), hs4(rt_, q, j),
                                     start=True, stop=True),
                                 wait=wt,
                                 inc=(("pe", 1),) if last else ())
                        xtra = dep("Loff", q) if dk == "D8t" else ()
                        round4(dk, q, smm, ev_to(dt_, dk),
                               dep(lk, q) + dep(rk, q) + xtra, 128)
                    th.append((0, cq))

            for q in range(4):
                def piq(q=q):
                    b.op("vector",
                         lambda e, q=q: e.tensor_tensor(
                             out=qview(P, q, 128), in0=qview(cc_, q, 128),
                             in1=identb[:].unsqueeze(1).broadcast_to(
                                 (128, 4, 128)), op=ALU.add),
                         wait=dep("Dt16", q) + (("ve", b.cnt["ve"]),
                                                ("pe", b.cnt["pe"])),
                         inc=(("ve", 1),))
                    evq[("P", q)] = ("ve", b.cnt["ve"])
                th.append((POST_R1, piq))
            for lk, lt in (("D8", ff_), ("D4", gg_), ("D2", ee_)):
                for q in range(4):
                    def ppq(lk=lk, lt=lt, q=q):
                        def pmm(j, out, wt, last, lt=lt, q=q):
                            b.op("tensor",
                                 lambda e, j=j, out=out, lt=lt, q=q:
                                 e.matmul(out, hs4(lt, q, j), hs4(P, q, j),
                                          start=True, stop=True),
                                 wait=wt,
                                 inc=(("pe", 1),) if last else ())

                        def pev(g, q, w):
                            pe_ = "vector"
                            ps_ = EV_SEM[pe_]
                            b.op(pe_,
                                 lambda e, g=g, q=q, w=w: e.tensor_tensor(
                                     out=qview(P, q, w), in0=qview(P, q, w),
                                     in1=pstr(grp_t(g), w), op=ALU.add),
                                 wait=(("pe", b.cnt["pe"]),) + dep("P", q),
                                 inc=((ps_, 1),))
                            evq[("P", q)] = (ps_, b.cnt[ps_])
                            for k in grp_banks(g):
                                bdone(k, ps_)
                        round4("P", q, pmm, pev, dep(lk, q) + dep("P", q),
                               128)
                    th.append((POST_R1, ppq))
            for q in range(4):
                def pfq(q=q):
                    def fmm(j, out, wt, last, q=q):
                        b.op("tensor",
                             lambda e, j=j, out=out, q=q: e.matmul(
                                 out, hs4(bb_, q, j), hs4(P, q, j),
                                 start=True, stop=True),
                             wait=wt, inc=(("pe", 1),) if last else ())

                    def fev(g, q, w):
                        pe_ = "vector"
                        ps_ = EV_SEM[pe_]
                        b.op(pe_,
                             lambda e, g=g, q=q, w=w: e.tensor_tensor(
                                 out=qview(P, q, w), in0=qview(P, q, w),
                                 in1=pstr(grp_t(g), w), op=ALU.subtract),
                             wait=(("pe", b.cnt["pe"]),) + dep("P", q),
                             inc=((ps_, 1),))
                        evq[("P", q)] = (ps_, b.cnt[ps_])
                        for k in grp_banks(g):
                            bdone(k, ps_)
                    round4("P", q, fmm, fev, dep("D", q) + dep("P", q), 128)
                th.append((POST_R1, pfq))
            for q in range(4):
                def mhq(q=q):
                    def mmm(j, out, wt, last, q=q):
                        b.op("tensor",
                             lambda e, j=j, out=out, q=q: e.matmul(
                                 out, hs4(dd_, q, j), hs4(P, q, j),
                                 start=True, stop=True),
                             wait=wt, inc=(("pe", 1),) if last else ())
                    round4("Mh", q, mmm, ev_to(ws["mhat"], "Mh"),
                           dep("Loff", q) + dep("P", q), 128)
                th.append((POST_U, mhq))
            if not state_only:
                for q in range(4):
                    def stq(q=q, Kf=Kf, Qf=Qf):
                        def stmm(j, out, wt, last, q=q):
                            h = 4 * q + j
                            b.op("tensor",
                                 lambda e, h=h, out=out: e.matmul(
                                     out, Kf2(Kf, h), Kf2(Qf, h),
                                     start=True, stop=True),
                                 wait=wt, inc=(("pe", 1),) if last else ())

                        def stev(g, q, w):
                            b.op("vector",
                                 lambda e, g=g, q=q, w=w: e.tensor_tensor(
                                     out=qview(ws["st"], q, w),
                                     in0=pstr(grp_t(g), w),
                                     in1=masks[:, M_STRIU:M_STRIU + 128
                                               ].unsqueeze(1).broadcast_to(
                                         (128, 4, w)), op=ALU.mult),
                                 wait=(("pe", b.cnt["pe"]),),
                                 inc=(("ve", 1),))
                            evq[("St", q)] = ("ve", b.cnt["ve"])
                            for k in grp_banks(g):
                                bdone(k, "ve")
                        round4("St", q, stmm, stev, ws["kf_ready"], 128)
                    th.append((post_o, stq))
            return th

        def plan_B(c, Kf, Vt, Qf, Ofun, state_only):
            th = []
            x0, x1, x2, x3 = ws["x0"], ws["x1"], ws["x2"], ws["x3"]
            kn_t = ws["kn"][c % 2]

            for q in range(4):
                def rq(q=q, Kf=Kf, Vt=Vt):
                    def rmm(j, out, wt, last, q=q):
                        h = 4 * q + j
                        b.op("tensor",
                             lambda e, h=h, out=out: e.matmul(
                                 out, Kf2(Kf, h), wslice(wst_bf, h),
                                 start=True, stop=True),
                             wait=wt, inc=(("pe", 1),) if last else ())

                    def rev(g, q, w):
                        b.op("vector",
                             lambda e, g=g, q=q, w=w: e.tensor_tensor(
                                 out=qview(x0, q, w), in0=pstr(grp_t(g), w),
                                 in1=eta16[:, 4 * q:4 * q + 4].unsqueeze(
                                     2).broadcast_to((128, 4, w)),
                                 op=ALU.mult),
                             wait=(("pe", b.cnt["pe"]),), inc=(("ve", 1),))
                        for k in grp_banks(g):
                            bdone(k, "ve")
                        b.op("vector",
                             lambda e, q=q, w=w: e.tensor_sub(
                                 out=x0[:, q * 256:(q + 1) * 256],
                                 in0=Vt[:, q * 256:(q + 1) * 256],
                                 in1=x0[:, q * 256:(q + 1) * 256]),
                             wait=(("ve", b.cnt["ve"]),) +
                             tuple(ws["vt_ready"]),
                             inc=(("ve", 1),))
                        evq[("R", q)] = ("ve", b.cnt["ve"])
                    round4("R", q, rmm, rev,
                           tuple(ws["w_ready"]) +
                           (("ve", ws["w_last_ve"][0]),), 64)
                th.append(rq)

            def apply_q(dst_key, dst, lam_t, lam_key, rho, rho_key, q,
                        extra_key=None):
                def amm(j, out, wt, last, q=q):
                    h = 4 * q + j
                    b.op("tensor",
                         lambda e, h=h, out=out: e.matmul(
                             out, hs(lam_t, h),
                             rho[:, h * 64:(h + 1) * 64], start=True,
                             stop=True),
                         wait=wt, inc=(("pe", 1),) if last else ())
                xtra = dep(extra_key, q) if extra_key else ()
                round4(dst_key, q, amm, ev_to(dst, dst_key),
                       dep(lam_key, q) + dep(rho_key, q) + xtra, 64)

            for q in range(4):
                th.append(lambda q=q: apply_q("R1", x1, P, "P", x0, "R", q))
            for q in range(4):
                th.append(lambda q=q: apply_q("b1", x2, ws["mhat"], "Mh",
                                              x1, "R1", q))
            for q in range(4):
                def zq(q=q):
                    b.op("gpsimd",
                         lambda e, q=q: e.tensor_sub(
                             out=x0[:, q * 256:(q + 1) * 256],
                             in0=x1[:, q * 256:(q + 1) * 256],
                             in1=x2[:, q * 256:(q + 1) * 256]),
                         wait=dep("b1", q) + dep("R1", q) +
                         (("po", b.cnt["po"]),),
                         inc=(("po", 1),))
                    evq[("z", q)] = ("po", b.cnt["po"])
                th.append(zq)
            for q in range(4):
                th.append(lambda q=q: apply_q("b2", x2, ws["mhat"], "Mh",
                                              x2, "b1", q, extra_key="z"))
            for q in range(4):
                def z2q(q=q):
                    b.op("gpsimd",
                         lambda e, q=q: e.tensor_add(
                             out=x0[:, q * 256:(q + 1) * 256],
                             in0=x0[:, q * 256:(q + 1) * 256],
                             in1=x2[:, q * 256:(q + 1) * 256]),
                         wait=dep("b2", q) + dep("z", q) +
                         (("po", b.cnt["po"]),),
                         inc=(("po", 1),))
                    evq[("z2", q)] = ("po", b.cnt["po"])
                th.append(z2q)
            for q in range(4):
                def uq(q=q):
                    def umm(j, out, wt, last, q=q):
                        h = 4 * q + j
                        b.op("tensor",
                             lambda e, h=h, out=out: e.matmul(
                                 out, hs(ws["mhat"], h),
                                 x2[:, h * 64:(h + 1) * 64], start=True,
                                 stop=True),
                             wait=wt, inc=(("pe", 1),) if last else ())

                    def uev(g, q, w):
                        b.op("vector",
                             lambda e, g=g, q=q, w=w: e.tensor_tensor(
                                 out=qview(x3, q, w), in0=qview(x0, q, w),
                                 in1=pstr(grp_t(g), w), op=ALU.subtract),
                             wait=(("pe", b.cnt["pe"]),) + dep("z2", q),
                             inc=(("ve", 1),))
                        evq[("U", q)] = ("ve", b.cnt["ve"])
                        for k in grp_banks(g):
                            bdone(k, "ve")
                    round4("U", q, umm, uev, dep("Mh", q) + dep("b2", q),
                           64)
                th.append(uq)

            if not state_only:
                for half in range(2):
                    def oh(half=half, Qf=Qf, Ofun=Ofun):
                        g = rr[0] % 2
                        rr[0] += 1
                        for pp in range(4):
                            p = half * 4 + pp
                            he, ho = 2 * p, 2 * p + 1
                            reg = q8(4 * g + pp)[:, 0:128]
                            wt = ()
                            if pp == 0:
                                wt = bwait(*grp_banks(g)) + depall("U") + \
                                    depall("St")
                            b.op("tensor",
                                 lambda e, he=he, reg=reg: e.matmul(
                                     reg[0:64, :], wslice(wst_bf, he),
                                     Kf2(Qf, he), start=True, stop=False,
                                     tile_position=(0, 0)),
                                 wait=wt)
                            b.op("tensor",
                                 lambda e, ho=ho, reg=reg: e.matmul(
                                     reg[64:128, :], wslice(wst_bf, ho),
                                     Kf2(Qf, ho), start=True, stop=False,
                                     tile_position=(64, 64),
                                     skip_group_check=True))
                            b.op("tensor",
                                 lambda e, he=he, reg=reg: e.matmul(
                                     reg[0:64, :],
                                     x3[:, he * 64:(he + 1) * 64],
                                     hs(ws["st"], he), start=False,
                                     stop=True, tile_position=(0, 0)))
                            b.op("tensor",
                                 lambda e, ho=ho, reg=reg: e.matmul(
                                     reg[64:128, :],
                                     x3[:, ho * 64:(ho + 1) * 64],
                                     hs(ws["st"], ho), start=False,
                                     stop=True, tile_position=(0, 64),
                                     skip_group_check=True),
                                 inc=(("pe", 1),) if pp == 3 else ())
                        Ofun(b.cnt["pe"], half, grp_t(g))
                        for k in grp_banks(g):
                            bdone(k, "ve")
                    th.append(oh)

            for half in range(2):
                def wh(half=half, c=c, kn_t=kn_t):
                    g = rr[0] % 2
                    rr[0] += 1
                    for pp in range(4):
                        p = half * 4 + pp
                        he, ho = 2 * p, 2 * p + 1
                        reg = q8(4 * g + pp)[:, 0:64]
                        wt = ()
                        if pp == 0:
                            wt = bwait(*grp_banks(g)) + depall("U") + (
                                ws["ve_kn"][c % 2],)
                        b.op("tensor",
                             lambda e, he=he, reg=reg: e.matmul(
                                 reg[0:64, :],
                                 kn_t[:, (he // 2) * 128:
                                      (he // 2) * 128 + 64],
                                 x3[:, he * 64:(he + 1) * 64],
                                 start=True, stop=True,
                                 tile_position=(0, 0)),
                             wait=wt)
                        b.op("tensor",
                             lambda e, ho=ho, reg=reg: e.matmul(
                                 reg[64:128, :],
                                 kn_t[:, (ho // 2) * 128 + 64:
                                      (ho // 2) * 128 + 128],
                                 x3[:, ho * 64:(ho + 1) * 64],
                                 start=True, stop=True,
                                 tile_position=(0, 64)),
                             inc=(("pe", 1),) if pp == 3 else ())
                    pv = b.cnt["pe"]
                    b.op("vector",
                         lambda e, g=g, half=half: e.tensor_tensor(
                             out=wst[:, half * 256:(half + 1) * 256
                                     ].rearrange("p (j x) -> p j x", x=64),
                             in0=wst[:, half * 256:(half + 1) * 256
                                     ].rearrange("p (j x) -> p j x", x=64),
                             in1=pstr(grp_t(g), 64), op=ALU.add),
                         wait=(("pe", pv), ("ve", ws["w_last_ve"][0]))
                         if half == 0 else (("pe", pv),
                                            ("ve", b.cnt["ve"])),
                         inc=(("ve", 1),))
                    for k in grp_banks(g):
                        bdone(k, "ve")
                    if half == 1:
                        b.op("vector",
                             lambda e: e.tensor_copy(out=wst_bf[:],
                                                     in_=wst[:]),
                             wait=(("ve", b.cnt["ve"]),), inc=(("ve", 1),))
                        ws["w_last_ve"][0] = b.cnt["ve"]
                th.append(wh)
            return th

        return plan_A, plan_B

    def run_pair(Bth, Ath):
        ai = 0
        nB = len(Bth)
        for bi in range(nB):
            Bth[bi]()
            done = bi + 1
            rem_b = nB - done
            if rem_b == 0:
                break
            quota = cdiv(len(Ath) - ai, rem_b + 1)
            while quota > 0 and ai < len(Ath) and Ath[ai][0] <= done:
                Ath[ai][1]()
                ai += 1
                quota -= 1
        while ai < len(Ath):
            Ath[ai][1]()
            ai += 1

    def Kf2(Kf, h):
        t = Kf(h // 2)
        return t[(h % 2) * 64:(h % 2) * 64 + 64, :]

    def mk_ws(base_kb, kn_key):
        o = base_kb
        ws = {}
        for nm in ("sa", "sb", "sc", "sd", "se_", "sf", "sg_"):
            ws[nm] = ar(o, 4096, BF16)
            o += 4
        ws["ttblk"] = ar(o, 4096, BF16); o += 4
        ws["mhat"] = ar(o, 4096, BF16); o += 4
        ws["st"] = ar(o, 4096, BF16); o += 4
        for nm in ("x0", "x1", "x2", "x3"):
            ws[nm] = ar(o, 2048, BF16)
            o += 2
        ws["kn"] = [ar(o, 2048, BF16), ar(o + 2, 2048, BF16)]
        o += 4
        ws["ve_kn"] = [("se", 0), ("se", 0)]
        ws["evq"] = {}
        ws["rr"] = [0]
        ws["er"] = [0]
        ws["kn_key"] = kn_key
        ws["kn_psum"] = q8(K8.index(kn_key)).bitcast(BF16)
        ws["w_last_ve"] = [0]
        ws["end_kb"] = o
        return ws

    ln1wb = ar(32, 4096, F32)
    ln1bb = ar(36, 4096, F32)
    rowbuf_b = ar(40, 8192, F32)
    b.dma("sync", rowbuf_b[0:1, 0:D], ln1w_row, "sc",
          wait=(("se", se_p0a), ("ve", ve_p0)))
    b.dma("sync", rowbuf_b[0:1, D:2 * D], ln1b_row, "sc")
    ldw = b.cnt["sc"]
    bc_mm(rowbuf_b[0:1, 0:D], ln1wb, D, (("sc", ldw), ("ve", ve_p0)))
    bc_mm(rowbuf_b[0:1, D:2 * D], ln1bb, D, (("sc", ldw),))
    se_ln1b = b.cnt["se"]

    wkt_p0b = ar(80, 16384, BF16)
    wvt_p0b = ar(96, 16384, BF16)

    def stream_weight_1024(dram_t, dst_bf, wait0, eta_scale=False):
        wr = dram_t.rearrange("(c p) o -> c p o", p=128)
        ses = []
        for c in range(8):
            slot = c % 2
            ws_ = "w" + str(slot)
            wv = [("se", ses[c - 2])] if c >= 2 else []
            if c >= 2:
                wv.append((ws_, b.cnt[ws_]))
            b.dma("sync", wstage[:, slot * 1024:(slot + 1) * 1024], wr[c],
                  ws_, wait=tuple(wv) + (tuple(wait0) if c <= 1 else ()))
            ld = b.cnt[ws_]
            if eta_scale:
                b.op("vector",
                     lambda e, c=c, slot=slot: e.tensor_tensor(
                         out=wstage[:, slot * 1024:(slot + 1) * 1024
                                    ].rearrange("p (h t) -> p h t", t=64),
                         in0=wstage[:, slot * 1024:(slot + 1) * 1024
                                    ].rearrange("p (h t) -> p h t", t=64),
                         in1=eta_rep(64), op=ALU.mult),
                     wait=((ws_, ld),), inc=(("ve", 1),))
                wv2 = (("ve", b.cnt["ve"]),)
            else:
                wv2 = ((ws_, ld),)
            b.op("scalar",
                 lambda e, c=c, slot=slot: e.copy(
                     out=dst_bf[:, c * 1024:(c + 1) * 1024],
                     in_=wstage[:, slot * 1024:(slot + 1) * 1024]),
                 wait=wv2, inc=(("se", 1),))
            ses.append(b.cnt["se"])
        return b.cnt["se"]

    se_wk = stream_weight_1024(wkt, wkt_p0b, [("se", se_ln1b)])
    se_wv = stream_weight_1024(wvt, wvt_p0b, [("se", se_wk)],
                               eta_scale=True)

    wsP = mk_ws(112, "B2")
    assert wsP["end_kb"] <= 176
    planP_A, planP_B = mk_ttt(wsP)

    xpr = xp_s.rearrange("(g p) d -> g p d", p=128)
    pe_h1tr = {}
    ve_h1 = {}
    for sg in range(4):
        for gg in range(4):
            g = sg * 4 + gg
            slot = g % 2
            sxs = "sx" + str(slot)
            wex = ((("ve", ve_h1[g - 2]), (sxs, b.cnt[sxs]))
                   if g >= 2 else
                   (("se", se_ln1b), ("ve", ve_p0), (sxs, b.cnt[sxs])))
            b.dma("sync", ar(40 + 4 * slot, 4096, F32), xpr[g], sxs,
                  wait=wex)
            ldg = b.cnt[sxs]
            xg = ar(40 + 4 * slot, 4096, F32)
            vd = emit_ln_group(xg, ln_sum[:, 0:1], ln_sum[:, 1:2],
                               ((sxs, ldg),), ar(48, 4096, F32),
                               extra=(("ve", ve_h1.get(g - 1, 0)),))
            xnp = ar(48, 4096, F32)
            h1tok = ar(52 + 2 * slot, 2048, BF16)
            wtr = (("pe", pe_h1tr[g - 2]),) if g >= 2 else ()
            vd2 = emit_ln_group(xnp, ln_sum[:, 0:1], ln_sum[:, 1:2],
                                (("ve", vd),), h1tok, extra=wtr)
            b.op("vector",
                 lambda e, h1tok=h1tok: e.tensor_tensor(
                     out=h1tok[:], in0=h1tok[:], in1=ln1wb[:], op=ALU.mult),
                 wait=(("ve", vd2), ("se", se_wv)), inc=(("ve", 1),))
            b.op("vector",
                 lambda e, h1tok=h1tok: e.tensor_tensor(
                     out=h1tok[:], in0=h1tok[:], in1=ln1bb[:], op=ALU.add),
                 wait=(("ve", b.cnt["ve"]),), inc=(("ve", 1),))
            ve_h1[g] = b.cnt["ve"]
            h1pf3 = ar(56, 8192, BF16).rearrange("p (c t) -> p c t", t=512)
            emit_tr_group(gg, h1tok, h1pf3, ve_h1[g], "B3",
                          q8(7).bitcast(BF16))
            pe_h1tr[g] = b.cnt["pe"]
        ve_h1sg = b.cnt["ve"]
        h1pf = ar(56, 8192, BF16)
        kpf = ar(64, 8192, BF16)
        vtp = ar(72, 8192, BF16)
        for p in range(8):
            k = "A" + str(p % 4)
            for c in range(8):
                b.op("tensor",
                     lambda e, p=p, c=c, k=k: e.matmul(
                         bq(p % 4), wkt_p0b[:, c * 1024 + p * 128:
                                            c * 1024 + (p + 1) * 128],
                         h1pf[:, c * 512:(c + 1) * 512],
                         start=(c == 0), stop=(c == 7)),
                     wait=(bwait(k) + ((("ve", ve_h1sg), ("se", se_wk))
                                       if p == 0 else ())) if c == 0 else (),
                     inc=(("pe", 1),) if c == 7 else ())
            pv = b.cnt["pe"]
            b.op("vector",
                 lambda e, p=p: e.tensor_copy(
                     out=kpf[:, p * 512:(p + 1) * 512], in_=bq(p % 4)),
                 wait=(("pe", pv),), inc=(("ve", 1),))
            bdone(k, "ve")
        ve_kpf = b.cnt["ve"]
        for gg in range(4):
            k4, k5 = "B0", "B1"
            for c in range(8):
                for half in range(2):
                    kk = k4 if half == 0 else k5
                    bank = q8(4) if half == 0 else q8(5)
                    b.op("tensor",
                         lambda e, gg=gg, c=c, half=half, bank=bank:
                         e.matmul(
                             bank[:],
                             h1pf[:, c * 512 + gg * 128:
                                  c * 512 + (gg + 1) * 128],
                             wvt_p0b[:, c * 1024 + half * 512:
                                     c * 1024 + (half + 1) * 512],
                             start=(c == 0), stop=(c == 7)),
                         wait=(bwait(kk) + ((("ve", ve_kpf),)
                                            if gg == 0 and c == 0 else ())
                               ) if c == 0 else (),
                         inc=(("pe", 1),) if c == 7 and half == 1 else ())
            pv = b.cnt["pe"]
            b.op("vector",
                 lambda e, gg=gg: e.tensor_copy(
                     out=vtp[:, gg * 1024:gg * 1024 + 512], in_=q8(4)),
                 wait=(("pe", pv),), inc=(("ve", 1),))
            b.op("vector",
                 lambda e, gg=gg: e.tensor_copy(
                     out=vtp[:, gg * 1024 + 512:(gg + 1) * 1024],
                     in_=q8(5)),
                 wait=(("ve", b.cnt["ve"]),), inc=(("ve", 1),))
            bdone(k4, "ve"), bdone(k5, "ve")
        ve_vtp = b.cnt["ve"]
        if TRUNC == "p0b_proj":
            break
        wsP["kf_ready"] = (("ve", ve_kpf),)
        wsP["vt_ready"] = (("ve", ve_vtp),)
        wsP["w_ready"] = ()

        def KfP(gg, kpf=kpf):
            return lambda p, gg=gg, kpf=kpf: kpf[
                :, p * 512 + gg * 128:p * 512 + (gg + 1) * 128]

        nch = 1 if TRUNC == "p0b_c1" else (4 if TRUNC == "p0b_c4" else 4)
        if TRUNC == "p0b_c1" and sg > 0:
            nch = 0
        run_pair([], planP_A(sg * 4, KfP(0), None, True) if nch else [])
        for gg in range(nch):
            Ath = (planP_A(sg * 4 + gg + 1, KfP(gg + 1), None, True)
                   if gg < nch - 1 else [])
            run_pair(planP_B(sg * 4 + gg, KfP(gg),
                             vtp[:, gg * 1024:(gg + 1) * 1024],
                             None, None, True), Ath)
    b.op("vector", lambda e: e.tensor_scalar_mul(out=wst[:], in0=wst[:],
                                                 scalar1=flag_cell[:]),
         wait=(("ve", wsP["w_last_ve"][0]),), inc=(("ve", 1),))
    b.op("vector", lambda e: e.tensor_copy(out=wst_bf[:], in_=wst[:]),
         wait=(("ve", b.cnt["ve"]),), inc=(("ve", 1),))
    ve_p0b = b.cnt["ve"]
    ve_p0b_end2 = ve_p0b
    if "wmid" in DEBUG:
        b.op("scalar", lambda e: e.copy(out=dbg_stage[:, 0:512],
                                        in_=wst[:]),
             wait=(("ve", ve_p0b), ("sg", b.cnt["sg"])), inc=(("se", 1),))
        b.dma("gpsimd", dbg["wmid"][:, 0:512], dbg_stage[:, 0:512], "sg",
              wait=(("se", b.cnt["se"]),))
        b.op("gpsimd", None, wait=(("sg", b.cnt["sg"]),))


    if TRUNC == "p0b":
        b.flush()
        return nc
    Y = ar(32, 65536, F32)
    Y3 = Y.rearrange("p (g d) -> p g d", d=D)
    w13bf = [ar(96, 16384, BF16), ar(112, 16384, BF16)]
    w2bf = [ar(128, 8192, BF16), ar(136, 8192, BF16)]
    hhat = ar(144, 16384, BF16)
    combTb = ar(160, 4096, BF16)
    cbc_sb = ar(164, 4096, BF16)
    scoresE = ar(168, 8192, F32)
    sc_tok = ar(176, 1024, F32)
    bw_tok = ar(177, 1024, F32)
    sel_tok = ar(178, 1024, F32)
    aux_tok = ar(179, 256, F32)
    bw_hit = ar(179.25, 1024, F32)
    hs1 = ar(180, 2048, BF16)
    t1s = ar(182, 2048, BF16)
    combrow_t = ar(176, 4096, BF16)

    ve_p0b_end = b.cnt["ve"]
    b.op("vector", lambda e: e.memset(Y[:], 0.0),
         wait=(("ve", ve_p0b_end),), inc=(("ve", 1),))
    ve_y0 = b.cnt["ve"]

    for tb in range(4):
        for c in range(8):
            b.op("tensor",
                 lambda e, tb=tb, c=c: e.matmul(
                     q8(tb)[0:16, :], gwt_bf[:, c * 16:(c + 1) * 16],
                     xnf3[:, c, tb * 512:(tb + 1) * 512],
                     start=(c == 0), stop=(c == 7)),
                 wait=(bwait(K8[tb]) + (("ve", ve_p0),)) if c == 0 else (),
                 inc=(("pe", 1),) if c == 7 else ())
        pv = b.cnt["pe"]
        b.op("scalar",
             lambda e, tb=tb: e.activation(
                 out=scoresE[0:16, tb * 512:(tb + 1) * 512],
                 in_=q8(tb)[0:16, :], func=AF.Sigmoid),
             wait=(("pe", pv),), inc=(("se", 1),))
        bdone(K8[tb], "se")
    se_scores = b.cnt["se"]
    tb_ps = q8(4)
    for g in range(NG):
        b.op("tensor",
             lambda e, g=g: e.matmul(
                 tb_ps[:, g * 16:(g + 1) * 16],
                 scoresE[0:16, g * 128:(g + 1) * 128],
                 identf[0:16, 0:16], is_transpose=True,
                 start=True, stop=True),
             wait=(bwait("B0") + (("se", se_scores),)) if g == 0 else (),
             inc=(("pe", 1),) if g == NG - 1 else ())
    pv = b.cnt["pe"]
    b.op("vector", lambda e: e.tensor_copy(out=sc_tok[:, 0:256],
                                           in_=tb_ps[:, 0:256]),
         wait=(("pe", pv),), inc=(("ve", 1),))
    bdone("B0", "ve")
    ve_sct = b.cnt["ve"]

    def v3(t, w=16):
        return t[:, 0:256].rearrange("p (g q) -> p g q", q=w)

    b.op("vector",
         lambda e: e.tensor_tensor(
             out=v3(bw_tok), in0=v3(sc_tok),
             in1=gateb_b[:, 0:16].unsqueeze(1).broadcast_to((128, 16, 16)),
             op=ALU.add),
         wait=(("ve", ve_sct),), inc=(("ve", 1),))
    b.op("vector", lambda e: e.memset(sel_tok[:, 0:256], 0.0),
         wait=(("ve", b.cnt["ve"]),), inc=(("ve", 1),))
    for it in range(4):
        b.op("vector",
             lambda e: e.tensor_reduce(out=aux_tok[:, 0:16], in_=v3(bw_tok),
                                       axis=AX.X, op=ALU.max),
             wait=(("ve", b.cnt["ve"]),), inc=(("ve", 1),))
        b.op("vector",
             lambda e: e.tensor_tensor(
                 out=v3(bw_hit), in0=v3(bw_tok),
                 in1=aux_tok[:, 0:16].unsqueeze(2).broadcast_to(
                     (128, 16, 16)),
                 op=ALU.is_ge),
             wait=(("ve", b.cnt["ve"]),), inc=(("ve", 1),))
        b.op("vector",
             lambda e: e.tensor_max(out=sel_tok[:, 0:256],
                                    in0=sel_tok[:, 0:256],
                                    in1=bw_hit[:, 0:256]),
             wait=(("ve", b.cnt["ve"]),), inc=(("ve", 1),))
        b.op("vector",
             lambda e: e.tensor_scalar(out=bw_hit[:, 0:256],
                                       in0=bw_hit[:, 0:256],
                                       scalar1=1e30, scalar2=None,
                                       op0=ALU.mult),
             wait=(("ve", b.cnt["ve"]),), inc=(("ve", 1),))
        b.op("vector",
             lambda e: e.tensor_sub(out=bw_tok[:, 0:256],
                                    in0=bw_tok[:, 0:256],
                                    in1=bw_hit[:, 0:256]),
             wait=(("ve", b.cnt["ve"]),), inc=(("ve", 1),))
    b.op("vector",
         lambda e: e.tensor_mul(out=bw_tok[:, 0:256], in0=sel_tok[:, 0:256],
                                in1=sc_tok[:, 0:256]),
         wait=(("ve", b.cnt["ve"]),), inc=(("ve", 1),))
    b.op("vector",
         lambda e: e.tensor_reduce(out=aux_tok[:, 16:32], in_=v3(bw_tok),
                                   axis=AX.X, op=ALU.add),
         wait=(("ve", b.cnt["ve"]),), inc=(("ve", 1),))
    b.op("vector",
         lambda e: e.reciprocal(out=aux_tok[:, 32:48],
                                in_=aux_tok[:, 16:32]),
         wait=(("ve", b.cnt["ve"]),), inc=(("ve", 1),))
    b.op("vector",
         lambda e: e.tensor_scalar_mul(out=aux_tok[:, 32:48],
                                       in0=aux_tok[:, 32:48], scalar1=2.5),
         wait=(("ve", b.cnt["ve"]),), inc=(("ve", 1),))
    b.op("vector",
         lambda e: e.tensor_tensor(
             out=v3(bw_tok), in0=v3(bw_tok),
             in1=aux_tok[:, 32:48].unsqueeze(2).broadcast_to((128, 16, 16)),
             op=ALU.mult),
         wait=(("ve", b.cnt["ve"]),), inc=(("ve", 1),))
    ve_comb = b.cnt["ve"]
    ct_ps = pB[0:16, :]
    for g in range(NG):
        b.op("tensor",
             lambda e, g=g: e.matmul(
                 ct_ps[:, g * 128:(g + 1) * 128],
                 bw_tok[:, g * 16:(g + 1) * 16], identf[:],
                 is_transpose=True, start=True, stop=True),
             wait=(bwait("B0", "B1", "B2", "B3") + (("ve", ve_comb),)
                   ) if g == 0 else (),
             inc=(("pe", 1),) if g == NG - 1 else ())
    pv = b.cnt["pe"]
    b.op("vector", lambda e: e.tensor_copy(out=combTb[0:16, :],
                                           in_=ct_ps[:]),
         wait=(("pe", pv),), inc=(("ve", 1),))
    for k in ("B0", "B1", "B2", "B3"):
        bdone(k, "ve")
    ve_combT = b.cnt["ve"]

    se_stream_prev = b.cnt["se"]
    pe_w13_used = {}
    pe_w2_used = {}
    pe_cbc_prev = 0
    ve_cbc = 0
    last_Y_ve = {g: ve_y0 for g in range(NG)}
    ve_hhat_done = 0

    def stream_w2e(idx, dst_bf, wait0):
        wr = w2et[idx].rearrange("(c p) o -> c p o", p=128)
        ses = []
        for c in range(4):
            slot = c % 2
            ws_ = "n" + str(slot)
            wv = [("se", ses[c - 2])] if c >= 2 else []
            if c >= 2:
                wv.append((ws_, b.cnt[ws_]))
            b.dma("sync", wstage[:, slot * 1024:(slot + 1) * 1024], wr[c],
                  ws_, wait=tuple(wv) + (tuple(wait0) if c <= 1 else ()))
            ld = b.cnt[ws_]
            b.op("scalar",
                 lambda e, c=c, slot=slot: e.copy(
                     out=dst_bf[:, c * 1024:(c + 1) * 1024],
                     in_=wstage[:, slot * 1024:(slot + 1) * 1024]),
                 wait=((ws_, ld),), inc=(("se", 1),))
            ses.append(b.cnt["se"])
        return b.cnt["se"]

    for e_i in range(NE):
        sl_ = e_i % 2
        gate_w13 = [("se", se_stream_prev)]
        if e_i >= 2:
            gate_w13.append(("pe", pe_w13_used[e_i - 2]))
        se_a = stream_weight_1024(w13t[e_i], w13bf[sl_], gate_w13)
        gate_w2 = [("se", se_a)]
        if e_i >= 2:
            gate_w2.append(("pe", pe_w2_used[e_i - 2]))
        se_stream_prev = stream_w2e(e_i, w2bf[sl_], gate_w2)
        se_cast = se_stream_prev
        if e_i < NE - 1:
            b.dma("gpsimd", combrow_t[0:1, :], combTb[e_i:e_i + 1, :], "sg",
                  wait=(("ve", ve_combT), ("pe", pe_cbc_prev)))
            sgc = b.cnt["sg"]
            for qq in range(4):
                b.op("tensor",
                     lambda e, qq=qq: e.matmul(
                         q8(qq), ones_bf[:],
                         combrow_t[0:1, qq * 512:(qq + 1) * 512],
                         start=True, stop=True),
                     wait=((("sg", sgc),) + bwait(K8[0], K8[1], K8[2],
                                                  K8[3])) if qq == 0 else (),
                     inc=(("pe", 1),) if qq == 3 else ())
            pvc = b.cnt["pe"]
            b.op("vector",
                 lambda e: e.tensor_copy(out=cbc_sb[:, 0:2048],
                                         in_=pA[:, 0:2048]),
                 wait=(("pe", pvc), ("po", b.cnt["po"])) +
                 ((("ve", ve_cbc),) if ve_cbc else ()),
                 inc=(("ve", 1),))
            for kk in (K8[0], K8[1], K8[2], K8[3]):
                bdone(kk, "ve")
            pe_cbc_prev = b.cnt["pe"]
            ve_cbc = b.cnt["ve"]
        for pp in range(4):
            for ph in range(2):
                bk0 = 4 * ph
                for c in range(8):
                    for ti in range(2):
                        oc = pp + 4 * ti
                        for ts in range(2):
                            bi = bk0 + 2 * ti + ts
                            wt = ()
                            if c == 0:
                                wt = bwait(K8[bi])
                                if pp == 0 and ph == 0 and ti == 0                                         and ts == 0:
                                    wt = wt + (("se", se_cast),
                                               ("ve", ve_cbc),
                                               ("ve", ve_y0))
                            b.op("tensor",
                                 lambda e, c=c, oc=oc, ph=ph, ts=ts, bi=bi,
                                 sl_=sl_: e.matmul(
                                     q8(bi),
                                     w13bf[sl_][:, c * 1024 + oc * 128:
                                                c * 1024 + (oc + 1) * 128],
                                     xnf3[:, c, (2 * ph + ts) * 512:
                                          (2 * ph + ts + 1) * 512],
                                     start=(c == 0), stop=(c == 7)),
                                 wait=wt,
                                 inc=(("pe", 1),) if (c == 7 and ti == 1
                                                      and ts == 1) else ())
                pv = b.cnt["pe"]
                for ts in range(2):
                    gb, ub = bk0 + ts, bk0 + 2 + ts
                    col = (2 * ph + ts) * 512
                    b.op("scalar",
                         lambda e, gb=gb, ts=ts: e.activation(
                             out=hs1[:, ts * 512:(ts + 1) * 512],
                             in_=q8(gb), func=AF.Silu),
                         wait=(("pe", pv), ("po", b.cnt["po"]),
                               ("ve", b.cnt["ve"])),
                         inc=(("se", 1),))
                    sev = b.cnt["se"]
                    bdone(K8[gb], "se")
                    if e_i < NE - 1:
                        b.op("gpsimd",
                             lambda e, ts=ts, col=col: e.tensor_tensor(
                                 out=t1s[:, ts * 512:(ts + 1) * 512],
                                 in0=hs1[:, ts * 512:(ts + 1) * 512],
                                 in1=cbc_sb[:, col:col + 512],
                                 op=ALU.mult),
                             wait=(("se", sev), ("ve", b.cnt["ve"])),
                             inc=(("po", 1),))
                        pov = b.cnt["po"]
                        src_t = t1s
                        wt2 = (("po", pov),)
                    else:
                        src_t = hs1
                        wt2 = (("se", sev),)
                    b.op("vector",
                         lambda e, ub=ub, ts=ts, col=col, src_t=src_t,
                         pp=pp:
                         e.tensor_tensor(
                             out=hhat[:, pp * 2048 + col:
                                      pp * 2048 + col + 512],
                             in0=src_t[:, ts * 512:(ts + 1) * 512],
                             in1=q8(ub), op=ALU.mult),
                         wait=wt2 + (("ve", b.cnt["ve"]),),
                         inc=(("ve", 1),))
                    bdone(K8[ub], "ve")
        pe_w13_used[e_i] = b.cnt["pe"]
        ve_hhat_done = b.cnt["ve"]
        po_hhat = b.cnt["po"]
        se_hhat = b.cnt["se"]
        if TRUNC == "p1g" and e_i == 0:
            if "hh" in DEBUG:
                for i in range(8):
                    b.op("scalar",
                         lambda e, i=i: e.copy(
                             out=dbg_stage[:, 0:1024],
                             in_=hhat[:, i * 1024:(i + 1) * 1024]),
                         wait=(("ve", ve_hhat_done), ("po", po_hhat),
                               ("se", se_hhat), ("sg", b.cnt["sg"])),
                         inc=(("se", 1),))
                    b.dma("gpsimd", dbg["hh"][:, i * 1024:(i + 1) * 1024],
                          dbg_stage[:, 0:1024], "sg",
                          wait=(("se", b.cnt["se"]),))
                b.op("gpsimd", None, wait=(("sg", b.cnt["sg"]),))
            b.flush()
            return nc
        for g in range(NG):
            j = g % 4
            pt = pA if j < 2 else pB
            pv_view = pt[:, (j % 2) * 1024:(j % 2) * 1024 + 1024]
            kks = (K8[2 * j], K8[2 * j + 1])
            for ci in range(4):
                for half in range(2):
                    wt = ()
                    if ci == 0 and half == 0:
                        wt = bwait(*kks)
                        if g == 0:
                            wt = wt + (("ve", ve_hhat_done),
                                       ("po", po_hhat), ("se", se_hhat))
                    b.op("tensor",
                         lambda e, ci=ci, half=half, g=g, pv_view=pv_view,
                         sl_=sl_: e.matmul(
                             pv_view[:, half * 512:(half + 1) * 512],
                             hhat[:, ci * 2048 + g * 128:
                                  ci * 2048 + (g + 1) * 128],
                             w2bf[sl_][:, ci * 1024 + half * 512:
                                       ci * 1024 + (half + 1) * 512],
                             start=(ci == 0), stop=(ci == 3)),
                         wait=wt,
                         inc=(("pe", 1),) if (ci == 3 and half == 1) else ())
            pv = b.cnt["pe"]
            b.op("vector",
                 lambda e, g=g, pv_view=pv_view: e.tensor_tensor(
                     out=Y3[:, g, :], in0=Y3[:, g, :], in1=pv_view,
                     op=ALU.add),
                 wait=(("pe", pv), ("ve", last_Y_ve[g])), inc=(("ve", 1),))
            last_Y_ve[g] = b.cnt["ve"]
            bdone(kks[0], "ve"), bdone(kks[1], "ve")
        pe_w2_used[e_i] = b.cnt["pe"]

    g_tok = ar(96, 32768, BF16)
    g3 = g_tok.rearrange("p (g d) -> p g d", d=D)
    for g in range(NG):
        b.op("scalar",
             lambda e, g=g: e.activation(out=g3[:, g, :], in_=Y3[:, g, :],
                                         func=AF.Silu),
             wait=(("ve", last_Y_ve[g]), ("pe", pe_w13_used[NE - 1])),
             inc=(("se", 1),))
    se_g = b.cnt["se"]
    if "g" in DEBUG:
        for i in range(16):
            b.op("scalar",
                 lambda e, i=i: e.copy(out=dbg_stage[:, 0:1024],
                                       in_=g3[:, i, :]),
                 wait=(("se", se_g), ("sg", b.cnt["sg"])), inc=(("se", 1),))
            b.dma("gpsimd", dbg["g"][:, i * 1024:(i + 1) * 1024],
                  dbg_stage[:, 0:1024], "sg", wait=(("se", b.cnt["se"]),))
        b.op("gpsimd", None, wait=(("sg", b.cnt["sg"]),))
    if TRUNC == "p1":
        b.flush()
        return nc

    kf_t = ar(32, 32768, BF16)
    vt_t = ar(64, 32768, BF16)
    wq_bf = ar(128, 16384, BF16)
    wk_bf = ar(144, 16384, BF16)
    wv_bf = ar(160, 16384, BF16)
    h1sg = ar(176, 8192, BF16)
    sqst = ar(184, 1024, BF16)

    pe_p1_end = b.cnt["pe"]
    gate0 = [("se", se_stream_prev), ("pe", pe_p1_end), ("se", se_g)]
    se_q = stream_weight_1024(wqt, wq_bf, gate0)
    se_k = stream_weight_1024(wkt, wk_bf, [("se", se_q)])
    se_v = stream_weight_1024(wvt, wv_bf, [("se", se_k)], eta_scale=True)

    ve_q_done = {}
    for sg in range(4):
        t0 = sg * 512
        for c in range(8):
            b.op("tensor",
                 lambda e, c=c, t0=t0: e.matmul(
                     q8(0)[0:1, :], onesc_bf[:], xnf3[:, c, t0:t0 + 512],
                     start=(c == 0), stop=(c == 7)),
                 wait=(bwait("A0") + ((("se", se_v), ("ve", ve_p0b_end))
                                     if sg == 0 and c == 0 else ())
                       ) if c == 0 else (),
                 inc=(("pe", 1),) if c == 7 else ())
        pe_sum = b.cnt["pe"]
        for c in range(8):
            b.op("scalar",
                 lambda e, c=c, t0=t0: e.activation(
                     out=sqst[:, 0:512], in_=xnf3[:, c, t0:t0 + 512],
                     func=AF.Square),
                 wait=(("se", b.cnt["se"]), ("pe", b.cnt["pe"])),
                 inc=(("se", 1),))
            sev = b.cnt["se"]
            b.op("tensor",
                 lambda e, c=c: e.matmul(
                     q8(1)[0:1, :], onesc_bf[:], sqst[:, 0:512],
                     start=(c == 0), stop=(c == 7)),
                 wait=(("se", sev),) + (bwait("A1") if c == 0 else ()),
                 inc=(("pe", 1),))
        pe_sq = b.cnt["pe"]
        mu_r = wstage[0:1, 0:512]
        m2_r = wstage[0:1, 512:1024]
        var_r = wstage[0:1, 1024:1536]
        rstd_r = wstage[0:1, 1536:2048]
        b.op("vector", lambda e: e.tensor_scalar_mul(
            out=mu_r, in0=q8(0)[0:1, :], scalar1=1.0 / D),
             wait=(("pe", pe_sum), ("se", se_v)), inc=(("ve", 1),))
        bdone("A0", "ve")
        b.op("vector", lambda e: e.tensor_scalar_mul(
            out=m2_r, in0=q8(1)[0:1, :], scalar1=1.0 / D),
             wait=(("pe", pe_sq), ("ve", b.cnt["ve"])), inc=(("ve", 1),))
        bdone("A1", "ve")
        b.op("vector", lambda e: e.tensor_mul(out=var_r, in0=mu_r,
                                              in1=mu_r),
             wait=(("ve", b.cnt["ve"]),), inc=(("ve", 1),))
        b.op("vector", lambda e: e.tensor_sub(out=var_r, in0=m2_r,
                                              in1=var_r),
             wait=(("ve", b.cnt["ve"]),), inc=(("ve", 1),))
        b.op("scalar", lambda e: e.activation(out=var_r, in_=var_r,
                                              func=AF.Sqrt,
                                              bias=eps_cell[0:1, :]),
             wait=(("ve", b.cnt["ve"]),), inc=(("se", 1),))
        b.op("vector", lambda e: e.reciprocal(out=rstd_r, in_=var_r),
             wait=(("se", b.cnt["se"]),), inc=(("ve", 1),))
        ve_row = b.cnt["ve"]
        b.op("tensor", lambda e: e.matmul(q8(0), ones_f[:], mu_r,
                                          start=True, stop=True),
             wait=(("ve", ve_row),) + bwait("A0"), inc=(("pe", 1),))
        b.op("tensor", lambda e: e.matmul(q8(1), ones_f[:], rstd_r,
                                          start=True, stop=True),
             wait=bwait("A1"), inc=(("pe", 1),))
        pe_bc = b.cnt["pe"]
        ve_ap = b.cnt["ve"]
        for c in range(8):
            b.op("vector",
                 lambda e, c=c, t0=t0: e.tensor_sub(
                     out=sqst[:, 0:512].bitcast(F32)[:, 0:256] if False
                     else h1sg[:, c * 512:(c + 1) * 512],
                     in0=xnf3[:, c, t0:t0 + 512], in1=q8(0)),
                 wait=(("pe", pe_bc), ("ve", ve_ap)), inc=(("ve", 1),))
            b.op("vector",
                 lambda e, c=c: e.scalar_tensor_tensor(
                     out=h1sg[:, c * 512:(c + 1) * 512],
                     in0=h1sg[:, c * 512:(c + 1) * 512],
                     scalar=ln1w_cs[:, c:c + 1], in1=q8(1),
                     op0=ALU.mult, op1=ALU.mult),
                 wait=(("ve", b.cnt["ve"]),), inc=(("ve", 1),))
            b.op("vector",
                 lambda e, c=c: e.tensor_scalar_add(
                     out=h1sg[:, c * 512:(c + 1) * 512],
                     in0=h1sg[:, c * 512:(c + 1) * 512],
                     scalar1=ln1b_cs[:, c:c + 1]),
                 wait=(("ve", b.cnt["ve"]),), inc=(("ve", 1),))
            ve_ap = b.cnt["ve"]
        bdone("A0", "ve"), bdone("A1", "ve")
        ve_h1r = b.cnt["ve"]
        for which, wbf, dstt in (("q", wq_bf, None), ("k", wk_bf, kf_t)):
            for p in range(8):
                for c in range(8):
                    b.op("tensor",
                         lambda e, p=p, c=c, wbf=wbf: e.matmul(
                             q8(p), wbf[:, c * 1024 + p * 128:
                                        c * 1024 + (p + 1) * 128],
                             h1sg[:, c * 512:(c + 1) * 512],
                             start=(c == 0), stop=(c == 7)),
                         wait=(bwait(K8[p]) + ((("ve", ve_h1r),)
                                               if p == 0 else ())
                               ) if c == 0 else (),
                         inc=(("pe", 1),) if (p == 7 and c == 7) else ())
            pv = b.cnt["pe"]
            for half in range(2):
                t = pA if half == 0 else pB
                if dstt is None:
                    dv = xnf.rearrange("p (j t) -> p j t", t=NT)[
                        :, half * 4:half * 4 + 4, t0:t0 + 512]
                else:
                    dv = dstt.rearrange("p (j t) -> p j t", t=NT)[
                        :, half * 4:half * 4 + 4, t0:t0 + 512]
                b.op("vector",
                     lambda e, dv=dv, t=t: e.tensor_copy(out=dv,
                                                         in_=pstr(t, 512)),
                     wait=(("pe", pv),) if half == 0 else
                     (("ve", b.cnt["ve"]),),
                     inc=(("ve", 1),))
            for k in K8:
                bdone(k, "ve")
        for gg in range(4):
            jj = gg % 4
            pvt = pA if jj < 2 else pB
            view = pvt[:, (jj % 2) * 1024:(jj % 2) * 1024 + 1024]
            kks = (K8[2 * jj], K8[2 * jj + 1])
            for c in range(8):
                for half in range(2):
                    b.op("tensor",
                         lambda e, gg=gg, c=c, half=half, view=view, t0=t0:
                         e.matmul(
                             view[:, half * 512:(half + 1) * 512],
                             h1sg[:, c * 512 + gg * 128:
                                  c * 512 + (gg + 1) * 128],
                             wv_bf[:, c * 1024 + half * 512:
                                   c * 1024 + (half + 1) * 512],
                             start=(c == 0), stop=(c == 7)),
                         wait=(bwait(*kks)) if (c == 0 and half == 0)
                         else (),
                         inc=(("pe", 1),) if (c == 7 and half == 1) else ())
            pv = b.cnt["pe"]
            b.op("vector",
                 lambda e, gg=gg, view=view, sg=sg: e.tensor_copy(
                     out=vt_t[:, (sg * 4 + gg) * 1024:
                              (sg * 4 + gg + 1) * 1024], in_=view),
                 wait=(("pe", pv),), inc=(("ve", 1),))
            bdone(kks[0], "ve"), bdone(kks[1], "ve")
        ve_q_done[sg] = b.cnt["ve"]
    ve_proj = b.cnt["ve"]
    if TRUNC == "p2proj":
        if "proj" in DEBUG:
            for nm, tt, base in (("q", xnf, 0), ("k", kf_t, 4096),
                                 ("v", vt_t, 8192)):
                for i in range(4):
                    b.op("scalar",
                         lambda e, tt=tt, i=i: e.copy(
                             out=dbg_stage[:, 0:1024],
                             in_=tt[:, i * 1024:(i + 1) * 1024]),
                         wait=(("ve", ve_proj), ("sg", b.cnt["sg"])),
                         inc=(("se", 1),))
                    b.dma("gpsimd",
                          dbg["proj"][:, base + i * 1024:
                                      base + (i + 1) * 1024],
                          dbg_stage[:, 0:1024], "sg",
                          wait=(("se", b.cnt["se"]),))
            b.op("gpsimd", None, wait=(("sg", b.cnt["sg"]),))
        b.flush()
        return nc

    wsM = mk_ws(128, "B2")
    assert wsM["end_kb"] <= 184
    wsM["w_last_ve"] = [ve_p0b_end2]
    wsM["kf_ready"] = (("ve", ve_proj),)
    wsM["vt_ready"] = (("ve", ve_proj),)
    wsM["w_ready"] = ()
    planM_A, planM_B = mk_ttt(wsM)

    def KfM(c16):
        return lambda p, c16=c16: kf_t[:, p * 2048 + c16 * 128:
                                       p * 2048 + (c16 + 1) * 128]

    def QfM(c16):
        return lambda p, c16=c16: xnf[:, p * 2048 + c16 * 128:
                                      p * 2048 + (c16 + 1) * 128]

    def OfM(c16):
        def ofun(pv, half, t, c16=c16):
            dv = xnf.rearrange("p (j t) -> p j t", t=NT)[
                :, half * 4:half * 4 + 4, c16 * 128:(c16 + 1) * 128]
            b.op("vector",
                 lambda e, dv=dv, t=t: e.tensor_copy(
                     out=dv, in_=pstr(t, 128)),
                 wait=(("pe", pv),), inc=(("ve", 1),))
        return ofun

    nch2 = 1 if TRUNC == "p2c1" else 16
    run_pair([], planM_A(0, KfM(0), QfM(0), False))
    for c16 in range(nch2):
        Ath = (planM_A(c16 + 1, KfM(c16 + 1), QfM(c16 + 1), False)
               if c16 < nch2 - 1 else [])
        run_pair(planM_B(c16, KfM(c16),
                         vt_t[:, c16 * 1024:(c16 + 1) * 1024],
                         QfM(c16), OfM(c16), False), Ath)
    ve_ttt = b.cnt["ve"]
    if "oattn" in DEBUG:
        for i in range(16):
            b.op("scalar",
                 lambda e, i=i: e.copy(out=dbg_stage[:, 0:1024],
                                       in_=xnf[:, i * 1024:(i + 1) * 1024]),
                 wait=(("ve", ve_ttt), ("sg", b.cnt["sg"])),
                 inc=(("se", 1),))
            b.dma("gpsimd", dbg["oattn"][:, i * 1024:(i + 1) * 1024],
                  dbg_stage[:, 0:1024], "sg", wait=(("se", b.cnt["se"]),))
        b.op("gpsimd", None, wait=(("sg", b.cnt["sg"]),))
    if TRUNC in ("p2", "p2c1"):
        b.flush()
        return nc

    def xaV(g):
        return ar(4 * g, 4096, F32)

    ln2f_all = ar(64, 32768, BF16)
    xnfM = ar(128, 32768, BF16)
    h_ob = ar(128, 16384, BF16)
    w1blk = [ar(144, 8192, BF16), ar(152, 8192, BF16)]
    wot_bf = ar(160, 16384, BF16)

    def w2s(i):
        return ar(160 + 2 * i, 2048, BF16)

    def xldE(i):
        return ar(144 + 4 * i, 4096, F32)

    b2b3 = ar(176, 4096, F32)
    ln2g = ar(180, 2048, BF16)
    ln2bt = ar(182, 2048, BF16)
    ln2tok = ar(184, 2048, BF16)

    b.op("vector", lambda e: e.tensor_copy(out=xnfM[:], in_=xnf[:]),
         wait=(("ve", b.cnt["ve"]), ("pe", b.cnt["pe"]),
               ("se", b.cnt["se"]), ("po", b.cnt["po"])),
         inc=(("ve", 1),))
    ve_rep = b.cnt["ve"]

    b.dma("sync", wstage[0:1, 0:D], ln2w_row, "sc", wait=(("ve", ve_rep),))
    se_r = bc_mm(wstage[0:1, 0:D], ln2g, D, (("sc", b.cnt["sc"]),))
    b.dma("sync", wstage[0:1, 0:D], ln2b_row, "sc", wait=(("se", se_r),))
    se_r = bc_mm(wstage[0:1, 0:D], ln2bt, D, (("sc", b.cnt["sc"]),))
    b.dma("sync", wstage[0:1, 0:D], b2_row, "sc", wait=(("se", se_r),))
    se_r = bc_mm(wstage[0:1, 0:D], b2b3, D, (("sc", b.cnt["sc"]),))
    se_wo = stream_weight_1024(wot, wot_bf, [("se", se_r)])

    ep_sems = ["ex0", "ex1", "ex2", "ex3"]
    for g in range(NG):
        sxs = ep_sems[g % 4]
        wx = ((sxs, b.cnt[sxs]),)
        if g < 4:
            wx += (("ve", ve_rep),)
        b.dma("gpsimd", xaV(g), xr[g], sxs, wait=wx)
        ldx = b.cnt[sxs]
        jj = g % 2
        pvt = pA if jj == 0 else pB
        view = pvt[:, 0:1024]
        kks = (K8[4 * jj], K8[4 * jj + 1])
        for c in range(8):
            for half in range(2):
                b.op("tensor",
                     lambda e, c=c, half=half, g=g, view=view: e.matmul(
                         view[:, half * 512:(half + 1) * 512],
                         xnfM[:, c * 2048 + g * 128:
                              c * 2048 + (g + 1) * 128],
                         wot_bf[:, c * 1024 + half * 512:
                                c * 1024 + (half + 1) * 512],
                         start=(c == 0), stop=(c == 7)),
                     wait=(bwait(*kks) + ((("se", se_wo), ("ve", ve_rep))
                                         if g == 0 else ())
                           ) if c == 0 and half == 0 else (),
                     inc=(("pe", 1),) if (c == 7 and half == 1) else ())
        pv = b.cnt["pe"]
        b.op("vector",
             lambda e, g=g: e.tensor_scalar(
                 out=xaV(g), in0=xaV(g), scalar1=mu_cols[:, g:g + 1],
                 scalar2=rstd_cols[:, g:g + 1], op0=ALU.subtract,
                 op1=ALU.mult),
             wait=((sxs, ldx), ("ve", b.cnt["ve"])), inc=(("ve", 1),))
        b.op("vector",
             lambda e, g=g, view=view: e.tensor_add(
                 out=xaV(g), in0=xaV(g), in1=view),
             wait=(("ve", b.cnt["ve"]), ("pe", pv)), inc=(("ve", 1),))
        bdone(kks[0], "ve"), bdone(kks[1], "ve")
        ve_xa = b.cnt["ve"]
        b.op("vector", lambda e, g=g: e.reduce_sum(
            out=ln_sum[:, 0:1], in_=xaV(g), axis=AX.X),
             wait=(("ve", ve_xa),), inc=(("ve", 1),))
        b.op("scalar", lambda e, g=g: e.activation(
            out=ln2tok[:, 0:1024], in_=xaV(g), func=AF.Square,
            accum_out=ln_sum[:, 1:2]),
             wait=(("ve", ve_xa), ("se", b.cnt["se"]),
                   ("pe", b.cnt["pe"])), inc=(("se", 1),))
        vs, ss = b.cnt["ve"], b.cnt["se"]
        b.op("vector", lambda e: e.tensor_scalar_mul(
            out=ln_sum[:, 0:1], in0=ln_sum[:, 0:1], scalar1=1.0 / D),
             wait=(("ve", vs),), inc=(("ve", 1),))
        b.op("vector", lambda e: e.tensor_scalar_mul(
            out=ln_sum[:, 2:3], in0=ln_sum[:, 1:2], scalar1=1.0 / D),
             wait=(("se", ss), ("ve", b.cnt["ve"])), inc=(("ve", 1),))
        b.op("vector", lambda e: e.tensor_mul(
            out=ln_sum[:, 3:4], in0=ln_sum[:, 0:1], in1=ln_sum[:, 0:1]),
             wait=(("ve", b.cnt["ve"]),), inc=(("ve", 1),))
        b.op("vector", lambda e: e.tensor_sub(
            out=ln_sum[:, 2:3], in0=ln_sum[:, 2:3], in1=ln_sum[:, 3:4]),
             wait=(("ve", b.cnt["ve"]),), inc=(("ve", 1),))
        b.op("scalar", lambda e: e.activation(
            out=ln_sum[:, 3:4], in_=ln_sum[:, 2:3], func=AF.Sqrt,
            bias=eps_cell[:]),
             wait=(("ve", b.cnt["ve"]),), inc=(("se", 1),))
        b.op("vector", lambda e: e.reciprocal(
            out=ln_sum[:, 1:2], in_=ln_sum[:, 3:4]),
             wait=(("se", b.cnt["se"]),), inc=(("ve", 1),))
        b.op("vector",
             lambda e, g=g: e.tensor_scalar(
                 out=ln2tok[:, 0:1024], in0=xaV(g),
                 scalar1=ln_sum[:, 0:1], scalar2=ln_sum[:, 1:2],
                 op0=ALU.subtract, op1=ALU.mult),
             wait=(("ve", b.cnt["ve"]), ("se", b.cnt["se"])),
             inc=(("ve", 1),))
        b.op("vector",
             lambda e: e.tensor_mul(out=ln2tok[:, 0:1024],
                                    in0=ln2tok[:, 0:1024], in1=ln2g[:]),
             wait=(("ve", b.cnt["ve"]),), inc=(("ve", 1),))
        b.op("vector",
             lambda e: e.tensor_add(out=ln2tok[:, 0:1024],
                                    in0=ln2tok[:, 0:1024], in1=ln2bt[:]),
             wait=(("ve", b.cnt["ve"]),), inc=(("ve", 1),))
        ve_l2 = b.cnt["ve"]
        knb = q8(6).bitcast(BF16)
        for c in range(8):
            b.op("tensor",
                 lambda e, c=c: e.matmul(
                     knb[:, c * 128:(c + 1) * 128],
                     ln2tok[:, c * 128:(c + 1) * 128], identb[:],
                     is_transpose=True, start=True, stop=True),
                 wait=(bwait("B2") + (("ve", ve_l2),)) if c == 0 else (),
                 inc=(("pe", 1),) if c == 7 else ())
        pv = b.cnt["pe"]
        b.op("vector",
             lambda e, g=g: e.tensor_copy(
                 out=ln2f_all.rearrange("p (c t) -> p c t", t=NT)[
                     :, :, g * 128:(g + 1) * 128],
                 in_=knb.rearrange("p (c t) -> p c t", t=128)),
             wait=(("pe", pv),), inc=(("ve", 1),))
        bdone("B2", "ve")
    ve_l2_all = b.cnt["ve"]
    pe_woall = b.cnt["pe"]

    w1r = w1t.rearrange("(c p) o -> c p o", p=128)
    w2r = w2t.rearrange("(c p) o -> c p o", p=128)
    pe_w1_done = {}
    pe_w2_done = {}
    se_w1cast = {}
    se_w2cast = {}
    se_item = []
    it_idx = [0]

    def emit_stream(ob):
        for kind, idx in [("w1", c) for c in range(8)] + [
                ("w2", k) for k in range(4)]:
            it = it_idx[0]
            it_idx[0] += 1
            slot = it % 2
            ws_ = "w" + str(slot)
            wv = [("se", se_item[it - 2]) if it >= 2 else ("se", se_wo)]
            wv.append((ws_, b.cnt[ws_]))
            if kind == "w1":
                src = w1r[idx][:, ob * 512:(ob + 1) * 512]
                stg = wstage[:, slot * 1024:slot * 1024 + 512]
            else:
                src = w2r[ob * 4 + idx]
                stg = wstage[:, slot * 1024:(slot + 1) * 1024]
            b.dma("sync", stg, src, ws_, wait=tuple(wv))
            ld = b.cnt[ws_]
            cw = [(ws_, ld)]
            if ob < 2:
                cw.append(("pe", pe_woall))
            elif kind == "w1":
                cw.append(("pe", pe_w1_done[ob - 2]))
            else:
                cw.append(("pe", pe_w2_done[ob - 2]))
            if kind == "w1":
                b.op("scalar",
                     lambda e, idx=idx, ob=ob, stg=stg: e.copy(
                         out=w1blk[ob % 2][:, idx * 512:(idx + 1) * 512],
                         in_=stg),
                     wait=tuple(cw), inc=(("se", 1),))
            else:
                b.op("scalar",
                     lambda e, idx=idx, ob=ob, stg=stg: e.copy(
                         out=w2s((ob * 4 + idx) % 8)[:], in_=stg),
                     wait=tuple(cw), inc=(("se", 1),))
            se_item.append(b.cnt["se"])
            if kind == "w1" and idx == 7:
                se_w1cast[ob] = b.cnt["se"]
            if kind == "w2" and idx == 3:
                se_w2cast[ob] = b.cnt["se"]

    so_after = {}
    ep_ld = {}
    emit_stream(0)
    emit_stream(1)
    for ob in range(8):
        if ob >= 2:
            emit_stream(ob)
        for wv_ in range(2):
            for r in range(8):
                oc = wv_ * 2 + r // 4
                tq = r % 4
                for c in range(8):
                    wt = ()
                    if c == 0:
                        wt = bwait(K8[r])
                        if r == 0 and wv_ == 0:
                            wt += (("se", se_w1cast[ob]),)
                            if ob == 0:
                                wt += (("ve", ve_l2_all),)
                    b.op("tensor",
                         lambda e, c=c, oc=oc, tq=tq, r=r, ob=ob: e.matmul(
                             q8(r),
                             w1blk[ob % 2][:, c * 512 + oc * 128:
                                           c * 512 + oc * 128 + 128],
                             ln2f_all[:, c * 2048 + tq * 512:
                                      c * 2048 + (tq + 1) * 512],
                             start=(c == 0), stop=(c == 7)),
                         wait=wt, inc=(("pe", 1),) if c == 7 else ())
                pv = b.cnt["pe"]
                b.op("scalar",
                     lambda e, oc=oc, tq=tq, r=r, ob=ob: e.activation(
                         out=h_ob[:, oc * 2048 + tq * 512:
                                  oc * 2048 + (tq + 1) * 512],
                         in_=q8(r), func=AF.Gelu,
                         bias=b1_cs[:, ob * 4 + oc:ob * 4 + oc + 1]),
                     wait=(("pe", pv),), inc=(("se", 1),))
                bdone(K8[r], "se")
        pe_w1_done[ob] = b.cnt["pe"]
        se_h = b.cnt["se"]
        for v in range(4):
            for r in range(8):
                tt = v * 4 + r // 2
                hh = r % 2
                for k in range(4):
                    wt = ()
                    if k == 0:
                        wt = bwait(K8[r])
                        if r == 0 and v == 0:
                            wt += (("se", se_w2cast[ob]), ("se", se_h))
                    b.op("tensor",
                         lambda e, k=k, tt=tt, hh=hh, r=r, ob=ob: e.matmul(
                             q8(r),
                             h_ob[:, k * 2048 + tt * 128:
                                  k * 2048 + (tt + 1) * 128],
                             w2s((ob * 4 + k) % 8)[:, hh * 512:
                                                   (hh + 1) * 512],
                             start=(k == 0), stop=(k == 3)),
                         wait=wt, inc=(("pe", 1),) if k == 3 else ())
                pv = b.cnt["pe"]
                b.op("vector",
                     lambda e, tt=tt, hh=hh, r=r: e.tensor_add(
                         out=xaV(tt)[:, hh * 512:(hh + 1) * 512],
                         in0=xaV(tt)[:, hh * 512:(hh + 1) * 512],
                         in1=q8(r)),
                     wait=(("pe", pv),), inc=(("ve", 1),))
                bdone(K8[r], "ve")
            if ob == 7:
                ve_fin = b.cnt["ve"]
                for gg in range(4):
                    g = v * 4 + gg
                    slot = g % 4
                    sxs = ep_sems[slot]
                    wx = [(sxs, b.cnt[sxs])]
                    if g < 4:
                        wx += [("pe", pe_w1_done[7]), ("se", se_w1cast[7])]
                    else:
                        wx += [("eo" + str((g - 4) % 2), so_after[g - 4])]
                    b.dma("gpsimd", xldE(slot), xr[g], sxs, wait=tuple(wx))
                    ep_ld[g] = b.cnt[sxs]
                    b.op("vector",
                         lambda e, g=g: e.tensor_add(
                             out=xaV(g), in0=xaV(g), in1=b2b3[:]),
                         wait=(("ve", ve_fin),), inc=(("ve", 1),))
                    b.op("vector",
                         lambda e, g=g: e.tensor_mul(
                             out=xaV(g), in0=xaV(g), in1=g3[:, g, :]),
                         wait=(("ve", b.cnt["ve"]),), inc=(("ve", 1),))
                    b.op("vector",
                         lambda e, g=g, slot=slot: e.tensor_add(
                             out=xldE(slot)[:], in0=xldE(slot)[:],
                             in1=xaV(g)),
                         wait=(("ve", b.cnt["ve"]), (sxs, ep_ld[g])),
                         inc=(("ve", 1),))
                    ve_out = b.cnt["ve"]
                    sos = "eo" + str(g % 2)
                    b.dma("gpsimd", orr[g], xldE(slot)[:], sos,
                          wait=(("ve", ve_out), (sos, b.cnt[sos])))
                    so_after[g] = b.cnt[sos]
        pe_w2_done[ob] = b.cnt["pe"]
    b.op("sync", None, wait=(("eo0", b.cnt["eo0"]),
                             ("eo1", b.cnt["eo1"])))



    b.flush()
    return nc


def _prep_host(inputs):
    f = np.ascontiguousarray
    w = {}
    w["wqt"] = f(inputs["L0_attn_wq"].T)
    w["wkt"] = f(inputs["L0_attn_wk"].T)
    w["wvt"] = f(inputs["L0_attn_wv"].T)
    w["wot"] = f(inputs["L0_attn_wo"].T)
    w["w1t"] = f(inputs["L0_attn_w1"].T)
    w["w2t"] = f(inputs["L0_attn_w2"].T)
    w["w13t"] = f(np.concatenate(
        [inputs["L0_ffn_w13"].transpose(0, 2, 1),
         inputs["L0_ffn_w13_s"].T[None]], axis=0))
    w["w2et"] = f(np.concatenate(
        [inputs["L0_ffn_w2"].transpose(0, 2, 1),
         inputs["L0_ffn_w2_s"].T[None]], axis=0))
    w["gate_wt"] = f(inputs["L0_ffn_gate_w"].T)
    w["ln1w_c"] = f(inputs["L0_attn_ln1_w"].reshape(8, 128).T)
    w["ln1b_c"] = f(inputs["L0_attn_ln1_b"].reshape(8, 128).T)
    w["ln1w_row"] = f(inputs["L0_attn_ln1_w"].reshape(1, -1))
    w["ln1b_row"] = f(inputs["L0_attn_ln1_b"].reshape(1, -1))
    w["ln2w_row"] = f(inputs["L0_attn_ln2_w"].reshape(1, -1))
    w["ln2b_row"] = f(inputs["L0_attn_ln2_b"].reshape(1, -1))
    w["b1_c"] = f(inputs["L0_attn_b1"].reshape(32, 128).T)
    w["b2_row"] = f(inputs["L0_attn_b2"].reshape(1, -1))
    w["gateb_col"] = f(inputs["L0_ffn_gate_b"].reshape(16, 1))
    w["gateb_row"] = f(inputs["L0_ffn_gate_b"].reshape(1, 16))
    w["lr_row"] = f(np.repeat(inputs["L0_attn_lr_scale"].reshape(16), 64
                              ).reshape(1, -1))
    i_, j_ = np.indices((128, 128))
    bi, bj = i_ // 32, j_ // 32
    w["stril"] = (i_ > j_).astype(np.float32)
    w["bdl"] = ((i_ > j_) & (bi == bj)).astype(np.float32)
    w["bdu"] = ((i_ < j_) & (bi == bj)).astype(np.float32)
    w["offu"] = ((i_ < j_) & (bi != bj)).astype(np.float32)
    w["offl"] = ((i_ > j_) & (bi != bj)).astype(np.float32)
    w["striu"] = (i_ < j_).astype(np.float32)
    w["ident_bf"] = np.eye(128).astype(ml_dtypes.bfloat16)
    w["ident_f"] = np.eye(128).astype(np.float32)
    return w


def kernel(**inputs):
    inputs = {k: np.asarray(v) for k, v in inputs.items()}
    x = np.ascontiguousarray(inputs["L0_attn_x"].astype(np.float32,
                                                        copy=False))
    shared = _prep_host(inputs)
    nc = build_program()
    in_maps = []
    ones_col = np.ones((128, 1), np.float32)
    zeros_col = np.zeros((128, 1), np.float32)
    for c in range(8):
        bb, hh = c // 2, c % 2
        m = dict(shared)
        m["x_s"] = np.ascontiguousarray(x[bb, hh * NT:(hh + 1) * NT])
        m["xp_s"] = np.ascontiguousarray(x[bb, 0:NT])
        m["wflag"] = ones_col if hh == 1 else zeros_col
        in_maps.append(m)
    res = run_bass_kernel_spmd(nc, in_maps, core_ids=list(range(8)))
    out = np.empty((4, 4096, 1024), np.float32)
    for c in range(8):
        bb, hh = c // 2, c % 2
        out[bb, hh * NT:(hh + 1) * NT] = res.results[c]["out_s"]
    return out
